# version 11
# baseline (speedup 1.0000x reference)
import math
from contextlib import ExitStack

import numpy as np
import ml_dtypes

import concourse.bass as bass
import concourse.mybir as mybir
from concourse.bass_utils import run_bass_kernel_spmd

F32 = mybir.dt.float32
BF16 = mybir.dt.bfloat16
AF = mybir.ActivationFunctionType
ALU = mybir.AluOpType
AX = mybir.AxisListType

E = 2048
NH = 16
HD = 128
CH = 128
NCORE = 8
SEG = 1024
NCHK = SEG // CH
KC = E // 128
NE = 8
F = 1408
FC = F // 128
GUC = 2 * FC
FS = 2816
FSC = FS // 128
SGUC = 2 * FSC
JW = 12288
EPS = 1e-5

ENGS = ("sync", "gpsimd", "tensor", "vector", "scalar")


def _slopes():
    start = 2.0 ** (-(2.0 ** -(math.log2(NH) - 3)))
    return np.array([start * start ** i for i in range(NH)], dtype=np.float64)


class Sem:
    __slots__ = ("h", "n")


class Bld:
    def __init__(self, nc, ctx):
        self.nc = nc
        self.ctx = ctx
        self.ops = {e: [] for e in ENGS}
        self.last_wait = {e: {} for e in ENGS}
        self.prev_tok = {}
        self.pending = {}
        self._sid = 0

    def sem(self):
        s = Sem()
        self._sid += 1
        s.h = self.ctx.enter_context(self.nc.semaphore(f"s{self._sid}"))
        s.n = 0
        return s

    def inject_all(self, tok):
        for e in ENGS:
            self.pending.setdefault(e, []).append(tok)

    def emit(self, eng, build, waits=(), inc=None):
        wl = []
        allw = list(waits)
        if self.pending.get(eng):
            allw.extend(self.pending.pop(eng))
        if eng in ("vector", "scalar") and self.prev_tok.get(eng):
            allw.append(self.prev_tok[eng])
        for w in allw:
            s, v = w if isinstance(w, tuple) else (w, w.n)
            if v > 0 and self.last_wait[eng].get(id(s), -1) < v:
                wl.append((s.h, v))
                self.last_wait[eng][id(s)] = v
        il = []
        if inc is not None:
            s, k = inc if isinstance(inc, tuple) else (inc, 1)
            s.n += k
            il.append((s.h, k))
            if eng in ("vector", "scalar"):
                self.prev_tok[eng] = (s, s.n)

        def fn(e, wl=wl, il=il, build=build):
            for h, v in wl:
                e.wait_ge(h, v)
            ins = build(e)
            for h, k in il:
                ins.then_inc(h, k)
        self.ops[eng].append(fn)

    def flush_block(self):
        ops = self.ops
        self.ops = {e: [] for e in ENGS}
        with self.nc.Block() as b:
            regs = {"sync": b.sync, "gpsimd": b.gpsimd, "tensor": b.tensor,
                    "vector": b.vector, "scalar": b.scalar}
            for name in ENGS:
                if ops[name]:
                    regs[name](lambda e, l=ops[name]: [f(e) for f in l])


def OP(name, *a, **k):
    return lambda e: getattr(e, name)(*a, **k)


class Arena:

    def __init__(self, nc, ctx, size=212832):
        self.nc = nc
        self.size = size
        self.backing = ctx.enter_context(
            nc.sbuf_tensor("arena", [128, size], mybir.dt.uint8))
        self.base = nc.lookup_mloc(self.backing).addr
        self.free_list = [(0, size)]
        self.live = {}

    def alloc(self, name, shape, dtype):
        per = 1
        for s in shape[1:]:
            per *= s
        per *= mybir.dt.size(dtype)
        per = (per + 31) & ~31
        for idx, (off, ln) in enumerate(self.free_list):
            if ln >= per:
                break
        else:
            live = sorted(self.live.items(), key=lambda kv: -kv[1])
            raise MemoryError(
                f"arena OOM for {name} ({per}B); live: " +
                ", ".join(f"{k}={v}" for k, v in live))
        self.free_list[idx] = (off + per, ln - per)
        if self.free_list[idx][1] == 0:
            del self.free_list[idx]
        hnd = self.nc.alloc_sbuf_tensor_at(name, shape, dtype,
                                           offset=self.base + off)
        self.live[name] = per
        return hnd, off, per

    def free(self, off, per, name=None):
        if name:
            self.live.pop(name, None)
        self.free_list.append((off, per))
        self.free_list.sort()
        merged = []
        for o, l in self.free_list:
            if merged and merged[-1][0] + merged[-1][1] == o:
                merged[-1] = (merged[-1][0], merged[-1][1] + l)
            else:
                merged.append((o, l))
        self.free_list = [tuple(m) for m in merged]

    def peak_used(self):
        free_now = sum(l for _, l in self.free_list)
        return self.size - free_now


def build_program():
    sl = _slopes()
    r = np.exp(-sl)
    rC = (r ** float(CH)).tolist()

    nc = bass.Bass("TRN2", target_bir_lowering=False, debug=False)

    DI = lambda n, s, dt=F32: nc.dram_tensor(n, s, dt, kind="ExternalInput").ap()
    x5 = DI("x5", [128, KC, SEG])
    wq5 = DI("wq5", [KC, 128, KC, 128])
    wk5 = DI("wk5", [KC, 128, KC, 128])
    wo5 = DI("wo5", [KC, 128, KC, 128])
    wv5 = DI("wv5", [4, 128, KC, 512])
    gup5 = DI("gup5", [NE, GUC, 128, KC, 128])
    dn5 = DI("dn5", [NE, KC, 128, FC, 128])
    sgu5 = DI("sgu5", [SGUC, 128, KC, 128])
    sdn5 = DI("sdn5", [KC, 128, FSC, 128])
    gw5 = DI("gw5", [128, KC, 8])
    c_kb5 = DI("c_kb", [128, KC])
    rmsw5 = DI("rmsw", [128, KC])
    ct5 = DI("ct", [128, SEG], BF16)
    st5 = DI("st", [128, SEG], BF16)
    dmt5 = DI("dmt", [128, NH, 128], BF16)
    qdb5 = DI("qdb", [128, NH, 128], BF16)
    kdc5 = DI("kdc", [128, NH])
    dsg5 = DI("dsg", [128, 3 * NH])
    rcb5 = DI("rcb", [128, NH * 128], BF16)
    idf5 = DI("idf", [128, 128])
    idb5 = DI("idb", [128, 128], BF16)
    swb5 = DI("swb", [128, 128], BF16)
    wmq5 = DI("wmq", [6, 128, KC, 512])

    out5 = nc.dram_tensor("out5", [128, KC, SEG], F32, kind="ExternalOutput").ap()

    agi = nc.dram_tensor("agi", [128, NH * HD], F32)
    ago = nc.dram_tensor("ago", [4 * 128, NH * HD], F32)
    mgi = nc.dram_tensor("mgi", [1, JW // 4], F32)
    mgo = nc.dram_tensor("mgo", [4, JW // 4], F32)
    xmd = nc.dram_tensor("xmd", [128, KC, SEG], F32)

    gctx = ExitStack()
    bd = Bld(nc, gctx)
    arena = Arena(nc, gctx)

    def SB(n, s, dt=F32, ctx=gctx):
        hnd, off, per = arena.alloc(n, s, dt)
        ctx.callback(arena.free, off, per, n)
        return hnd

    def PSUM(n, s, dt=F32, ctx=gctx):
        return ctx.enter_context(nc.psum_tensor(n, s, dt))

    identf = SB("identf", [128, 128])
    identb = SB("identb", [128, 128], BF16)
    swapb = SB("swapb", [128, 128], BF16)
    kdec = SB("kdec", [128, NH])
    dsg = SB("dsg_s", [128, 3 * NH])
    rmsw = SB("rmsw_s", [128, KC])
    c_kb = SB("ckb_s", [128, KC])
    s_kbt = SB("s_kbt", [128, KC], BF16)
    gw_sb = SB("gw_sb", [128, KC * 8], BF16)
    onesf = SB("onesf", [1, 128])
    onesb = SB("onesb", [128, 1], BF16)
    ones1b = SB("ones1b", [1, 128], BF16)
    ca1 = SB("ca1", [128, KC])
    sav = SB("sav", [128, KC])
    gav = SB("gav", [128, KC])
    cb1 = SB("cb1", [128, KC])
    sbv = SB("sbv", [128, KC])
    gbv = SB("gbv", [128, KC])
    gscr = SB("gscr", [128, KC])
    sgn = SB("sgn", [128, KC])
    wbuf = [SB(f"wbuf{i}", [128, KC * 128], BF16) for i in range(2)]

    def mk_rows(ctx, tag):
        d = {}
        for nm in ("t1", "t2", "sig", "n1"):
            d[nm] = SB(f"r_{nm}{tag}", [1, 512], F32, ctx)
        for nm in ("mu", "is_", "b"):
            d[nm] = [SB(f"r_{nm}{tag}{i}", [1, 512], F32, ctx) for i in range(2)]
        return d

    s_in = bd.sem()
    s_gin = bd.sem()
    s_vz = bd.sem()
    vch = bd.sem()
    sch = bd.sem()
    s_st = bd.sem()
    s_wb = [bd.sem() for _ in range(2)]
    s_vb = [bd.sem() for _ in range(2)]
    s_db = [bd.sem() for _ in range(4)]
    s_pe = bd.sem()
    s_ev = bd.sem()
    s_evs = bd.sem()
    s_evu = bd.sem()
    s_sq = bd.sem()
    s_row = bd.sem()
    s_h = bd.sem()

    def rowmath(R, ssum_ps, ssq_ps, tile, w_pe, n, rms_only=False):
        o_mu = None if rms_only else R["mu"][tile]
        o_is = R["is_"][tile]
        w0 = [(s_pe, w_pe)]
        if not rms_only:
            bd.emit("vector", OP("tensor_scalar_mul", o_mu[:], ssum_ps[:], 1.0 / n),
                    waits=w0 + [vch], inc=vch)
        bd.emit("vector", OP("tensor_scalar_mul", R["t1"][:], ssq_ps[:], 1.0 / n),
                waits=w0 + [vch], inc=vch)
        if not rms_only:
            bd.emit("vector", OP("tensor_mul", R["t2"][:], o_mu[:], o_mu[:]),
                    waits=[vch], inc=vch)
            bd.emit("vector", OP("tensor_sub", R["t1"][:], R["t1"][:], R["t2"][:]),
                    waits=[vch], inc=vch)
        bd.emit("vector", OP("tensor_scalar_add", R["t2"][:], R["t1"][:], EPS),
                waits=[vch], inc=vch)
        bd.emit("scalar", OP("activation", R["sig"][:], R["t2"][:], AF.Sqrt),
                waits=[vch, sch], inc=sch)
        bd.emit("vector", OP("reciprocal", o_is[:], R["sig"][:]),
                waits=[sch], inc=vch)
        bd.emit("vector", OP("tensor_mul", R["n1"][:], o_is[:], o_is[:]),
                waits=[vch], inc=vch)
        bd.emit("vector", OP("tensor_mul", R["n1"][:], R["n1"][:], R["t2"][:]),
                waits=[vch], inc=vch)
        bd.emit("vector", OP("tensor_scalar", R["n1"][:], R["n1"][:], -0.5, 1.5,
                             ALU.mult, ALU.add), waits=[vch], inc=vch)
        if rms_only:
            bd.emit("vector", OP("tensor_mul", o_is[:], o_is[:], R["n1"][:]),
                    waits=[vch], inc=s_row)
        else:
            bd.emit("vector", OP("tensor_mul", o_is[:], o_is[:], R["n1"][:]),
                    waits=[vch], inc=vch)
            bd.emit("vector", OP("scalar_tensor_tensor", R["b"][tile][:], o_mu[:],
                                 -1.0, o_is[:], ALU.mult, ALU.mult),
                    waits=[vch], inc=s_row)

    ctx12 = ExitStack()
    ctx1m = ExitStack()
    ctx_rot = ExitStack()
    x_bf = SB("x_bf", [128, KC * SEG], BF16, ctx12)
    ct_sb = SB("ct_sb", [128, SEG], BF16, ctx_rot)
    st_sb = SB("st_sb", [128, SEG], BF16, ctx_rot)
    ctx_bb = ExitStack()
    bigbuf = [SB(f"bigbuf{i}", [128, KC * 512], BF16, ctx_bb) for i in range(2)]
    xsq = [SB(f"xsq{i}", [128, 512], BF16, ctx12) for i in range(2)]
    RA = mk_rows(ctx12, "a")
    mrowf = SB("mrowf", [1, JW // 4], F32, ctx1m)
    mvr = SB("mvr", [16, 6 * 128], F32, ctx1m)

    ctx_ps1 = ExitStack()
    mps = [PSUM(f"mps{i}", [1, 512], F32, ctx_ps1) for i in range(2)]
    mtr = PSUM("mtr", [128, 16], F32, ctx_ps1)
    ssum = PSUM("ssum", [1, 512], F32, ctx_ps1)
    ssq = PSUM("ssq", [1, 512], F32, ctx_ps1)

    for dst, src in ((identf, idf5), (kdec, kdc5), (dsg, dsg5), (rmsw, rmsw5),
                     (c_kb, c_kb5), (identb, idb5), (swapb, swb5),
                     (ct_sb, ct5), (st_sb, st5)):
        bd.emit("sync", OP("dma_start", out=dst[:], in_=src), inc=(s_in, 16))
    n_in1 = s_in.n
    bd.emit("gpsimd", OP("dma_start",
                         out=x_bf[:].rearrange("p (c t) -> p c t", c=KC), in_=x5),
            inc=(s_gin, 16))
    bd.emit("gpsimd", OP("dma_start",
                         out=gw_sb[:].rearrange("p (c m) -> p c m", c=KC), in_=gw5),
            inc=(s_gin, 16))
    n_gin1 = s_gin.n

    bd.emit("vector", OP("memset", onesf[:], 1.0), inc=s_vz)
    bd.emit("vector", OP("memset", onesb[:], 1.0), inc=s_vz)
    bd.emit("vector", OP("memset", ones1b[:], 1.0), inc=s_vz)

    bd.emit("scalar", OP("activation", s_kbt[:], c_kb[:], AF.Silu),
            waits=[(s_in, n_in1), sch], inc=sch)
    n_silu_c = sch.n

    mv_used = [0, 0]
    mev = [0, 0]
    s_mcc = bd.sem()
    for js in range(6):
        bi = js % 2
        bd.emit("gpsimd", OP("dma_start",
                             out=bigbuf[bi][:].rearrange("p (c j) -> p c j", c=KC),
                             in_=wmq5[js]),
                waits=[(s_pe, mv_used[bi])], inc=(s_vb[bi], 16))

        def mv(e, bi=bi, nvb=s_vb[bi].n, nev=mev[bi], nsch=n_silu_c):
            e.wait_ge(s_vb[bi].h, nvb)
            e.wait_ge(sch.h, nsch)
            if nev:
                e.wait_ge(s_ev.h, nev)
            ins = None
            for k in range(KC):
                ins = e.matmul(mps[bi][:], s_kbt[:, k:k + 1],
                               bigbuf[bi][:, k * 512:(k + 1) * 512],
                               start=(k == 0), stop=(k == KC - 1))
            return ins
        bd.emit("tensor", mv, inc=s_pe)
        mv_used[bi] = s_pe.n
        bd.emit("vector", OP("tensor_copy", mrowf[:, js * 512:(js + 1) * 512],
                             mps[bi][:]),
                waits=[(s_pe, s_pe.n), vch], inc=s_ev)
        mev[bi] = s_ev.n
    n_mrow = s_ev.n

    s_md = bd.sem()
    bd.emit("sync", OP("dma_start", out=mgi.ap(), in_=mrowf[:, 0:JW // 4]),
            waits=[(s_ev, n_mrow)], inc=(s_md, 16))

    def mcoll(e):
        return e.collective_compute(
            "AllGather", ALU.bypass,
            replica_groups=[[0, 1, 2, 3], [4, 5, 6, 7]],
            ins=[mgi.ap().opt()], outs=[mgo.ap().opt()])
    bd.emit("gpsimd", mcoll, waits=[(s_md, 16)], inc=s_mcc)
    bd.emit("sync", OP("dma_start",
                       out=mvr[:].rearrange("c (v p) -> c v p", v=6),
                       in_=mgo.ap().rearrange("q r -> (q r)").rearrange(
                           "(v c p) -> c v p", v=6, c=16)),
            waits=[(s_mcc, 1)], inc=(s_md, 16))

    s_mv = bd.sem()
    for vi, nm in enumerate(["sa", "ca", "ga", "sb", "cb", "gb"]):
        bd.emit("tensor", OP("transpose", mtr[:], mvr[:, vi * 128:(vi + 1) * 128],
                             identf[0:16, 0:16]),
                waits=[(s_md, 32), (s_in, n_in1), (s_mv, vi)], inc=s_pe)
        w = [(s_pe, s_pe.n)]
        if nm in ("sa", "sb"):
            dst = sav if nm == "sa" else sbv
            bd.emit("vector", OP("tensor_copy", dst[:], mtr[:]),
                    waits=w + [vch], inc=s_mv)
        elif nm in ("ca", "cb"):
            dst = ca1 if nm == "ca" else cb1
            bd.emit("vector", OP("tensor_scalar_add", dst[:], mtr[:], 1.0),
                    waits=w + [vch], inc=s_mv)
        else:
            dst = gav if nm == "ga" else gbv
            bd.emit("vector", OP("tensor_copy", gscr[:], mtr[:]),
                    waits=w + [vch], inc=vch)
            bd.emit("vector", OP("tensor_scalar", sgn[:], gscr[:], 0.0, None,
                                 ALU.is_ge), waits=[vch], inc=vch)
            bd.emit("vector", OP("tensor_scalar", sgn[:], sgn[:], 2.0, -1.0,
                                 ALU.mult, ALU.add), waits=[vch], inc=vch)
            bd.emit("scalar", OP("activation", dst[:], gscr[:], AF.Abs),
                    waits=[vch, sch], inc=sch)
            bd.emit("vector", OP("tensor_scalar_max", dst[:], dst[:], 0.25),
                    waits=[sch, vch], inc=vch)
            bd.emit("vector", OP("tensor_mul", dst[:], dst[:], sgn[:]),
                    waits=[vch], inc=s_mv)
    n_modv = s_mv.n

    pair_tok = {}
    for tile in range(2):
        for k in range(KC):
            bi = k % 2
            c0 = k * SEG + tile * 512
            bd.emit("scalar", OP("activation", xsq[bi][:],
                                 x_bf[:, c0:c0 + 512], AF.Square),
                    waits=[(s_gin, n_gin1),
                           (s_pe, pair_tok.get((tile, k - 2),
                                               pair_tok.get((tile - 1, KC - 1), 0)))],
                    inc=s_sq)
            nsq = s_sq.n

            def statk(e, k=k, c0=c0, nsq=nsq, tile=tile):
                e.wait_ge(s_sq.h, nsq)
                e.matmul(ssum[:], onesb[:], x_bf[:, c0:c0 + 512],
                         start=(k == 0), stop=(k == KC - 1))
                return e.matmul(ssq[:], onesb[:], xsq[k % 2][:],
                                start=(k == 0), stop=(k == KC - 1))
            w = [(s_gin, n_gin1), s_vz]
            if k == 0:
                w.append((s_row, s_row.n))
            bd.emit("tensor", statk, waits=w, inc=s_pe)
            pair_tok[(tile, k)] = s_pe.n
        rowmath(RA, ssum, ssq, tile, s_pe.n, E)

    bd.flush_block()
    ctx_ps1.close()
    ctx1m.close()

    ctx_ps2 = ExitStack()
    pq = [PSUM(f"pq{i}", [128, 512], F32, ctx_ps2) for i in range(2)]
    qsw = [PSUM(f"qsw{i}", [128, 512], F32, ctx_ps2) for i in range(2)]
    ctx2s = ExitStack()
    A_ps = PSUM("A_ps", [128, 512], F32, ctx2s)
    B_ps = PSUM("B_ps", [128, 512], F32, ctx2s)
    ctx23 = ExitStack()
    ctx24 = ExitStack()
    h_bf = SB("h_bf", [128, KC * SEG], BF16, ctx23)
    k_sb = SB("k_sb", [128, NH * SEG], BF16, ctx24)
    v_sb = SB("v_sb", [128, NCHK * E], BF16, ctx24)
    t1s = SB("t1s", [128, 512], F32, ctx23)
    t2s = SB("t2s", [128, 512], F32, ctx23)
    qraw = [SB(f"qraw{i}", [128, 512], BF16, ctx23) for i in range(2)]
    wrot = [SB(f"wrot{i}", [128, 512], BF16, ctx23) for i in range(2)]

    for tile in range(2):
        bd.emit("tensor", OP("matmul", A_ps[:], onesf[:], RA["is_"][tile][:],
                             start=True, stop=True),
                waits=[(s_row, tile + 1), (s_h, s_h.n), (s_in, n_in1)],
                inc=s_pe)
        bd.emit("tensor", OP("matmul", B_ps[:], onesf[:], RA["b"][tile][:],
                             start=True, stop=True), inc=s_pe)
        nab = s_pe.n
        for k in range(KC):
            c0 = k * SEG + tile * 512
            bd.emit("vector", OP("tensor_mul", t1s[:], x_bf[:, c0:c0 + 512],
                                 A_ps[:]),
                    waits=[(s_pe, nab), (s_mv, n_modv), vch], inc=vch)
            bd.emit("vector", OP("tensor_add", t2s[:], t1s[:], B_ps[:]),
                    waits=[vch], inc=vch)
            bd.emit("vector", OP("tensor_scalar", h_bf[:, c0:c0 + 512], t2s[:],
                                 ca1[:, k:k + 1], sav[:, k:k + 1],
                                 ALU.mult, ALU.add),
                    waits=[vch], inc=s_h)
    n_h_all = s_h.n
    n_h_tile = [16, 32]
    ctx2s.close()
    ctx12.close()

    ctx34 = ExitStack()
    kvsv = SB("kvsv", [128, NCHK * NH * HD], BF16, ctx34)
    kvf = SB("kvf", [128, NH * HD], F32, ctx34)
    kvin = SB("kvin", [128, NH * HD], F32, ctx34)
    ctx_u7 = ExitStack()
    u7 = SB("u7", [128, NH * HD], BF16, ctx_u7)
    ctx3o = ExitStack()
    ktd = [SB(f"ktd{i}", [128, 128], BF16, ctx3o) for i in range(4)]

    ctx_ps3 = ExitStack()
    tp_ps = [PSUM(f"tp{i}", [128, 128], BF16, ctx_ps3) for i in range(2)]
    U_ps = [PSUM(f"U{i}", [128, 128], F32, ctx_ps3) for i in range(2)]

    s_ag = bd.sem()
    s_cc = bd.sem()
    s_lp = bd.sem()
    s_kvin = bd.sem()

    pq_free = [0, 0]
    qsw_free = [None, None]
    qraw_free = [0, 0]
    rot_ready = {}
    wb_used = [0, 0]

    def proj_rot(w5, dst_sb, dst_name, skip_dma=0):
        for ms in range(16):
            bi = ms % 2
            if ms >= skip_dma:
                bd.emit("gpsimd", OP("dma_start",
                                     out=wbuf[bi][:].rearrange("p (c m) -> p c m", c=KC),
                                     in_=w5[ms]),
                        waits=[(s_pe, wb_used[bi])], inc=(s_wb[bi], 16))
            for tile in range(2):
                i2 = (ms * 2 + tile) % 2
                c0 = ms * SEG + tile * 512

                def mmg(e, bi=bi, tile=tile, i2=i2, nwb=s_wb[bi].n,
                        nev=pq_free[i2]):
                    e.wait_ge(s_wb[bi].h, nwb)
                    e.wait_ge(s_h.h, n_h_tile[tile])
                    if nev:
                        e.wait_ge(s_evs.h, nev)
                    ins = None
                    for k in range(KC):
                        ins = e.matmul(pq[i2][:],
                                       wbuf[bi][:, k * 128:(k + 1) * 128],
                                       h_bf[:, k * SEG + tile * 512:
                                            k * SEG + tile * 512 + 512],
                                       start=(k == 0), stop=(k == KC - 1))
                    return ins
                bd.emit("tensor", mmg, inc=s_pe)
                n_mm = s_pe.n
                bd.emit("scalar", OP("copy", qraw[i2][:], pq[i2][:]),
                        waits=[(s_pe, n_mm), (s_ev, qraw_free[i2])], inc=s_evs)
                pq_free[i2] = s_evs.n
                n_evac = s_evs.n

                def swm(e, i2=i2, nev=n_evac, nfree=qsw_free[i2]):
                    e.wait_ge(s_evs.h, nev)
                    if nfree:
                        e.wait_ge(nfree[0].h, nfree[1])
                    return e.matmul(qsw[i2][:], swapb[:], qraw[i2][:],
                                    start=True, stop=True)
                bd.emit("tensor", swm, waits=[(s_in, n_in1)], inc=s_pe)
                n_sw = s_pe.n
                bd.emit("vector", OP("tensor_mul", wrot[i2][:], qsw[i2][:],
                                     st_sb[:, tile * 512:tile * 512 + 512]),
                        waits=[(s_pe, n_sw), vch], inc=vch)
                qsw_free[i2] = (vch, vch.n)
                bd.emit("vector", OP("tensor_mul", qraw[i2][:], qraw[i2][:],
                                     ct_sb[:, tile * 512:tile * 512 + 512]),
                        waits=[(s_evs, n_evac)], inc=vch)
                bd.emit("vector", OP("tensor_add", dst_sb[:, c0:c0 + 512],
                                     qraw[i2][:], wrot[i2][:]),
                        waits=[vch], inc=s_ev)
                rot_ready[(dst_name, ms, tile)] = s_ev.n
                qraw_free[i2] = s_ev.n
            wb_used[bi] = s_pe.n

    proj_rot(wk5, k_sb, "k")

    vb_used = [0, 0]
    pv_free = [0, 0]
    v_tok = {}
    pa_tok = {}
    tp_free = [0, 0]
    ktd_free = [0, 0, 0, 0]
    u_free = [0, 0, 0, 0]
    pai = [0]
    kvsv0_tok = [0]

    def u_bank(i4):
        return U_ps[i4][:] if i4 < 2 else qsw[i4 - 2][:, 0:128]

    def phase_a(es, tch):
        for hh in range(4):
            h = es * 4 + hh
            i2 = pai[0] % 2
            i4 = pai[0] % 4
            pai[0] += 1
            ks = h * SEG + tch * 128

            def tpm(e, ks=ks, i2=i2, nk=rot_ready[("k", h, tch // 4)],
                    nfree=tp_free[i2]):
                e.wait_ge(s_ev.h, nk)
                if nfree:
                    e.wait_ge(s_evu.h, nfree)
                return e.transpose(tp_ps[i2][:], k_sb[:, ks:ks + 128], identb[:])
            bd.emit("tensor", tpm, inc=s_pe)
            bd.emit("scalar", OP("activation", ktd[i4][:], tp_ps[i2][:],
                                 AF.Identity, scale=kdec[:, h:h + 1]),
                    waits=[(s_pe, s_pe.n), (s_pe, ktd_free[i4])],
                    inc=s_evu)
            tp_free[i2] = s_evu.n
            n_ktd = s_evu.n

            def umm(e, tch=tch, h=h, i4=i4, nktd=n_ktd, nu=u_free[i4],
                    nv=v_tok[(es, tch)],
                    nrot=(qsw_free[i4 - 2] if i4 >= 2 else None)):
                e.wait_ge(s_evu.h, max(nktd, nu))
                e.wait_ge(s_evs.h, nv)
                if nrot:
                    e.wait_ge(nrot[0].h, nrot[1])
                return e.matmul(u_bank(i4), ktd[i4][:],
                                v_sb[:, tch * E + h * 128:tch * E + h * 128 + 128],
                                start=True, stop=True)
            bd.emit("tensor", umm, inc=s_pe)
            ktd_free[i4] = s_pe.n
            dst = (kvsv[:, ((tch + 1) * NH + h) * 128:((tch + 1) * NH + h) * 128 + 128]
                   if tch < NCHK - 1 else u7[:, h * 128:h * 128 + 128])
            bd.emit("scalar", OP("copy", dst, u_bank(i4)),
                    waits=[(s_pe, s_pe.n)], inc=s_evu)
            u_free[i4] = s_evu.n
            pa_tok[(tch, h)] = s_evu.n

    def phase_b(es):
        for h in range(es * 4, es * 4 + 4):
            for c in range(1, NCHK):
                o0 = (c * NH + h) * 128
                p0 = ((c - 1) * NH + h) * 128
                bd.emit("vector", OP("scalar_tensor_tensor",
                                     kvsv[:, o0:o0 + 128],
                                     kvsv[:, p0:p0 + 128], float(rC[h]),
                                     kvsv[:, o0:o0 + 128], ALU.mult, ALU.add),
                        waits=[(s_evu, pa_tok[(c - 1, h)]),
                               (s_ev, kvsv0_tok[0]), vch], inc=vch)
            o7 = ((NCHK - 1) * NH + h) * 128
            bd.emit("vector", OP("scalar_tensor_tensor",
                                 kvf[:, h * 128:h * 128 + 128],
                                 kvsv[:, o7:o7 + 128], float(rC[h]),
                                 u7[:, h * 128:h * 128 + 128],
                                 ALU.mult, ALU.add),
                    waits=[(s_evu, pa_tok[(NCHK - 1, h)]), vch], inc=s_ev)

    for es in range(4):
        bi = es % 2
        bd.emit("gpsimd", OP("dma_start",
                             out=bigbuf[bi][:].rearrange("p (c j) -> p c j", c=KC),
                             in_=wv5[es]),
                waits=[(s_pe, vb_used[bi])], inc=(s_vb[bi], 16))
        for tch in range(NCHK):
            i2 = (es * NCHK + tch) % 2

            def vmm(e, bi=bi, tch=tch, i2=i2, nvb=s_vb[bi].n, nev=pv_free[i2]):
                e.wait_ge(s_vb[bi].h, nvb)
                e.wait_ge(s_h.h, n_h_tile[0] if tch < 4 else n_h_tile[1])
                if nev:
                    e.wait_ge(s_evs.h, nev)
                ins = None
                for k in range(KC):
                    ins = e.matmul(pq[i2][:],
                                   h_bf[:, k * SEG + tch * 128:
                                        k * SEG + tch * 128 + 128],
                                   bigbuf[bi][:, k * 512:(k + 1) * 512],
                                   start=(k == 0), stop=(k == KC - 1))
                return ins
            bd.emit("tensor", vmm, inc=s_pe)
            bd.emit("scalar", OP("copy",
                                 v_sb[:, tch * E + es * 512:tch * E + es * 512 + 512],
                                 pq[i2][:]),
                    waits=[(s_pe, s_pe.n)], inc=s_evs)
            pv_free[i2] = s_evs.n
            v_tok[(es, tch)] = s_evs.n
            if tch == 0 and es == 0:
                bd.emit("vector", OP("memset", kvsv[:, 0:NH * HD], 0.0),
                        waits=[vch], inc=s_ev)
                kvsv0_tok[0] = s_ev.n
                bd.emit("vector", OP("memset", kvin[:], 0.0), inc=vch)
            if tch >= 2:
                phase_a(es, tch - 2)
        phase_a(es, NCHK - 2)
        phase_a(es, NCHK - 1)
        phase_b(es)
        bd.emit("sync", OP("dma_start",
                           out=agi.ap()[:, es * 512:(es + 1) * 512],
                           in_=kvf[:, es * 512:(es + 1) * 512]),
                waits=[(s_ev, s_ev.n)], inc=(s_ag, 16))
        vb_used[bi] = s_pe.n
    n_v_all = s_evs.n
    n_scan = s_ev.n
    qsw_free[0] = (s_evu, u_free[2])
    qsw_free[1] = (s_evu, u_free[3])

    for ms in range(2):
        bi = ms % 2
        bd.emit("gpsimd", OP("dma_start",
                             out=wbuf[bi][:].rearrange("p (c m) -> p c m", c=KC),
                             in_=wq5[ms]),
                waits=[(s_pe, wb_used[bi])], inc=(s_wb[bi], 16))

    def coll(e):
        return e.collective_compute(
            "AllGather", ALU.bypass,
            replica_groups=[[0, 1, 2, 3], [4, 5, 6, 7]],
            ins=[agi.ap().opt()], outs=[ago.ap().opt()])
    bd.emit("gpsimd", coll, waits=[(s_ag, 64)], inc=s_cc)

    ctx_bb.close()
    q_sb = SB("q_sb", [128, NH * SEG], BF16, ctx34)
    ctx_u7.close()
    lprev = SB("lprev", [128, NH * HD], BF16, ctx3o)

    proj_rot(wq5, q_sb, "q", skip_dma=2)

    for s2 in range(3):
        bd.emit("gpsimd", OP("dma_start", out=lprev[:],
                             in_=ago.ap()[s2 * 128:(s2 + 1) * 128, :]),
                waits=[(s_cc, 1), (s_kvin, s_kvin.n)], inc=(s_lp, 16))
        for h in range(NH):
            bd.emit("vector", OP("scalar_tensor_tensor",
                                 kvin[:, h * 128:h * 128 + 128],
                                 lprev[:, h * 128:h * 128 + 128],
                                 dsg[:, s2 * NH + h:s2 * NH + h + 1],
                                 kvin[:, h * 128:h * 128 + 128],
                                 ALU.mult, ALU.add),
                    waits=[(s_lp, 16 * (s2 + 1)), vch], inc=s_kvin)
    n_kvin = s_kvin.n

    bd.flush_block()
    ctx_ps3.close()
    ctx_ps2.close()
    ctx23.close()
    ctx3o.close()
    ctx_rot.close()

    ctx4 = ExitStack()
    attn = SB("attn", [128, NH * SEG], BF16, ctx4)
    dmt_sb = SB("dmt_sb", [128, NH * 128], BF16, ctx4)
    qdb_sb = SB("qdb_sb", [128, NH * 128], BF16, ctx4)
    rcb_sb = SB("rcb_sb", [128, NH * 128], BF16, ctx4)
    kvrun = SB("kvrun", [128, NH * 128], BF16, ctx4)
    kvt4 = [SB(f"kvt4_{i}", [128, 512], BF16, ctx4) for i in range(2)]
    qef = [SB(f"qef{i}", [128, 128], BF16, ctx4) for i in range(4)]
    smx = [SB(f"smx{i}", [128, 128], BF16, ctx4) for i in range(4)]

    ctx_ps4 = ExitStack()
    S_ps = [PSUM(f"S{i}", [128, 128], F32, ctx_ps4) for i in range(4)]
    o_ps = [PSUM(f"o{i}", [128, 128], F32, ctx_ps4) for i in range(4)]

    bd.emit("sync", OP("dma_start",
                       out=dmt_sb[:].rearrange("p (h i) -> p h i", h=NH), in_=dmt5),
            inc=(s_in, 16))
    bd.emit("sync", OP("dma_start",
                       out=qdb_sb[:].rearrange("p (h i) -> p h i", h=NH), in_=qdb5),
            inc=(s_in, 16))
    bd.emit("sync", OP("dma_start",
                       out=rcb_sb[:].rearrange("p (h i) -> p h i", h=NH), in_=rcb5),
            inc=(s_in, 16))
    n_in4 = s_in.n

    kvt4_free = [0, 0]
    qef_free = [0] * 4
    smx_free = [0] * 4
    S_free = [0] * 4
    o_free = [0] * 4
    attn_tok = {}
    bd.emit("vector", OP("tensor_copy", kvrun[:], kvin[:]),
            waits=[(s_kvin, n_kvin), vch], inc=vch)
    for c in range(NCHK):
        if c > 0:
            bd.emit("vector", OP("tensor_mul", kvrun[:], kvrun[:], rcb_sb[:]),
                    waits=[(s_in, n_in4), vch], inc=vch)
        for hg in range(4):
            g2 = (c * 4 + hg) % 2
            bd.emit("vector", OP("tensor_add", kvt4[g2][:],
                                 kvrun[:, hg * 512:hg * 512 + 512],
                                 kvsv[:, c * NH * HD + hg * 512:
                                      c * NH * HD + hg * 512 + 512]),
                    waits=[(s_pe, kvt4_free[g2]), vch], inc=vch)
            n_kvt = vch.n
            for hh in range(4):
                h = hg * 4 + hh
                i2 = (c * NH + h) % 4
                ks = h * SEG + c * 128
                bd.emit("vector", OP("tensor_mul", qef[i2][:],
                                     q_sb[:, ks:ks + 128],
                                     qdb_sb[:, h * 128:h * 128 + 128]),
                        waits=[(s_in, n_in4), (s_pe, qef_free[i2]),
                               (s_ev, rot_ready[("q", h, c // 4)]), vch],
                        inc=vch)
                n_qef = vch.n

                def smm(e, ks=ks, i2=i2, nfree=S_free[i2],
                        nq=rot_ready[("q", h, c // 4)]):
                    e.wait_ge(s_ev.h, nq)
                    if nfree:
                        e.wait_ge(vch.h, nfree)
                    return e.matmul(S_ps[i2][:], k_sb[:, ks:ks + 128],
                                    q_sb[:, ks:ks + 128], start=True, stop=True)
                bd.emit("tensor", smm, inc=s_pe)
                n_s = s_pe.n
                bd.emit("vector", OP("tensor_mul", smx[i2][:], S_ps[i2][:],
                                     dmt_sb[:, h * 128:h * 128 + 128]),
                        waits=[(s_pe, n_s), (s_pe, smx_free[i2]),
                               (s_in, n_in4), vch],
                        inc=vch)
                S_free[i2] = vch.n
                n_sm = vch.n

                def omm(e, c=c, h=h, hh=hh, g2=g2, i2=i2, n_sm=n_sm,
                        n_kvt=n_kvt, n_qef=n_qef, noe=o_free[i2]):
                    e.wait_ge(vch.h, max(n_sm, n_kvt, n_qef))
                    if noe:
                        e.wait_ge(sch.h, noe)
                    e.matmul(o_ps[i2][:],
                             v_sb[:, c * E + h * 128:c * E + h * 128 + 128],
                             smx[i2][:], start=True, stop=False)
                    return e.matmul(o_ps[i2][:],
                                    kvt4[g2][:, hh * 128:hh * 128 + 128],
                                    qef[i2][:], start=False, stop=True)
                bd.emit("tensor", omm, inc=s_pe)
                kvt4_free[g2] = s_pe.n
                qef_free[i2] = s_pe.n
                smx_free[i2] = s_pe.n
                bd.emit("scalar", OP("copy", attn[:, ks:ks + 128], o_ps[i2][:]),
                        waits=[(s_pe, s_pe.n), sch], inc=sch)
                o_free[i2] = sch.n

    bd.flush_block()
    ctx_ps4.close()
    ctx24.close()
    ctx34.close()

    ctx56 = ExitStack()
    x_mid = SB("x_mid", [128, KC * SEG], F32, ctx56)
    RB = mk_rows(ctx56, "b")
    ctx5 = ExitStack()
    attn_n = SB("attn_n", [128, NH * SEG], BF16, ctx5)
    xsq5b = [SB(f"xsq5b{i}", [128, 512], BF16, ctx5) for i in range(2)]
    xres = [SB(f"xres{i}", [128, 512], F32, ctx5) for i in range(2)]

    ctx_ps5 = ExitStack()
    ssq5 = PSUM("ssq5", [1, 512], F32, ctx_ps5)
    A5 = [PSUM(f"A5{i}", [128, 512], F32, ctx_ps5) for i in range(2)]
    po = [PSUM(f"po{i}", [128, 512], F32, ctx_ps5) for i in range(2)]

    s_xr = [bd.sem() for _ in range(2)]
    s_xm = bd.sem()

    n_row0 = s_row.n
    pair5 = {}
    for tile in range(2):
        for k in range(KC):
            bi = k % 2
            c0 = k * SEG + tile * 512
            bd.emit("scalar", OP("activation", xsq5b[bi][:], attn[:, c0:c0 + 512],
                                 AF.Square),
                    waits=[(sch, sch.n),
                           (s_pe, pair5.get((tile, k - 2),
                                            pair5.get((tile - 1, KC - 1), 0)))],
                    inc=s_sq)
            nsq = s_sq.n

            def stat5k(e, k=k, nsq=nsq):
                e.wait_ge(s_sq.h, nsq)
                return e.matmul(ssq5[:], onesb[:], xsq5b[k % 2][:],
                                start=(k == 0), stop=(k == KC - 1))
            w = [s_vz]
            if k == 0:
                w.append((s_row, s_row.n))
            bd.emit("tensor", stat5k, waits=w, inc=s_pe)
            pair5[(tile, k)] = s_pe.n
        rowmath(RB, None, ssq5, tile, s_pe.n, E, rms_only=True)
        bd.emit("tensor", OP("matmul", A5[tile][:], onesf[:], RB["is_"][tile][:],
                             start=True, stop=True),
                waits=[(s_row, n_row0 + tile + 1)], inc=s_pe)
        nA = s_pe.n
        for k in range(KC):
            c0 = k * SEG + tile * 512
            bd.emit("vector", OP("scalar_tensor_tensor", attn_n[:, c0:c0 + 512],
                                 attn[:, c0:c0 + 512], rmsw[:, k:k + 1],
                                 A5[tile][:], ALU.mult, ALU.mult),
                    waits=[(s_pe, nA), (s_in, n_in1), vch], inc=s_h)
    n_attn_n = s_h.n

    po_free = [0, 0]
    xr_used = [0, 0]
    for ms in range(16):
        bi = ms % 2
        bd.emit("gpsimd", OP("dma_start",
                             out=wbuf[bi][:].rearrange("p (c m) -> p c m", c=KC),
                             in_=wo5[ms]),
                waits=[(s_pe, wb_used[bi])], inc=(s_wb[bi], 16))
        for tile in range(2):
            i2 = (ms * 2 + tile) % 2
            bd.emit("sync", OP("dma_start", out=xres[i2][:],
                               in_=x5[:, ms:ms + 1,
                                      tile * 512:tile * 512 + 512]
                               .rearrange("p one t -> p (one t)")),
                    waits=[(s_xm, xr_used[i2])], inc=(s_xr[i2], 16))
            nxr = s_xr[i2].n

            def womm(e, bi=bi, tile=tile, i2=i2, nwb=s_wb[bi].n,
                     nev=po_free[i2]):
                e.wait_ge(s_wb[bi].h, nwb)
                e.wait_ge(s_h.h, n_attn_n)
                if nev:
                    e.wait_ge(s_xm.h, nev)
                ins = None
                for k in range(KC):
                    ins = e.matmul(po[i2][:], wbuf[bi][:, k * 128:(k + 1) * 128],
                                   attn_n[:, k * SEG + tile * 512:
                                          k * SEG + tile * 512 + 512],
                                   start=(k == 0), stop=(k == KC - 1))
                return ins
            bd.emit("tensor", womm, inc=s_pe)
            c0 = ms * SEG + tile * 512
            bd.emit("vector", OP("scalar_tensor_tensor", x_mid[:, c0:c0 + 512],
                                 po[i2][:], gav[:, ms:ms + 1], xres[i2][:],
                                 ALU.mult, ALU.add),
                    waits=[(s_pe, s_pe.n), (s_xr[i2], nxr), (s_mv, n_modv),
                           vch],
                    inc=s_xm)
            po_free[i2] = s_xm.n
            xr_used[i2] = s_xm.n
        wb_used[bi] = s_pe.n
    n_xmid = s_xm.n

    bd.flush_block()
    ctx_ps5.close()
    ctx5.close()
    ctx4.close()

    ctx67 = ExitStack()
    h2_bf = SB("h2_bf", [128, KC * SEG], BF16, ctx67)
    ctx6 = ExitStack()
    xb6 = [SB(f"xb6{i}", [128, 512], BF16, ctx6) for i in range(2)]
    xq6 = [SB(f"xq6{i}", [128, 512], BF16, ctx6) for i in range(2)]
    t16 = SB("t16", [128, 512], F32, ctx6)
    t26 = SB("t26", [128, 512], F32, ctx6)

    ctx_ps6 = ExitStack()
    ssum6 = PSUM("ssum6", [1, 512], F32, ctx_ps6)
    ssq6 = PSUM("ssq6", [1, 512], F32, ctx_ps6)
    A6 = PSUM("A6", [128, 512], F32, ctx_ps6)
    B6 = PSUM("B6", [128, 512], F32, ctx_ps6)

    s_h2 = bd.sem()
    s_spl = bd.sem()
    bd.emit("sync", OP("dma_start", out=xmd.ap(),
                       in_=x_mid[:].rearrange("p (c t) -> p c t", c=KC)),
            waits=[(s_xm, s_xm.n)], inc=(s_spl, 16))
    n_row6 = s_row.n
    pair6 = {}
    for tile in range(2):
        for k in range(KC):
            bi = k % 2
            c0 = k * SEG + tile * 512
            ptok = pair6.get((tile, k - 2), pair6.get((tile - 1, KC - 1), 0))
            bd.emit("vector", OP("tensor_copy", xb6[bi][:], x_mid[:, c0:c0 + 512]),
                    waits=[(s_xm, n_xmid), (s_pe, ptok)], inc=vch)
            ncp = vch.n
            bd.emit("scalar", OP("activation", xq6[bi][:], x_mid[:, c0:c0 + 512],
                                 AF.Square),
                    waits=[(s_pe, ptok)], inc=s_sq)
            nsq = s_sq.n

            def stat6k(e, k=k, ncp=ncp, nsq=nsq):
                e.wait_ge(vch.h, ncp)
                e.wait_ge(s_sq.h, nsq)
                e.matmul(ssum6[:], onesb[:], xb6[k % 2][:],
                         start=(k == 0), stop=(k == KC - 1))
                return e.matmul(ssq6[:], onesb[:], xq6[k % 2][:],
                                start=(k == 0), stop=(k == KC - 1))
            w = [s_vz]
            if k == 0:
                w.append((s_row, s_row.n))
            bd.emit("tensor", stat6k, waits=w, inc=s_pe)
            pair6[(tile, k)] = s_pe.n
        rowmath(RB, ssum6, ssq6, tile, s_pe.n, E)
        bd.emit("tensor", OP("matmul", A6[:], onesf[:], RB["is_"][tile][:],
                             start=True, stop=True),
                waits=[(s_row, n_row6 + tile + 1), (s_h2, s_h2.n)], inc=s_pe)
        bd.emit("tensor", OP("matmul", B6[:], onesf[:], RB["b"][tile][:],
                             start=True, stop=True), inc=s_pe)
        nab = s_pe.n
        for k in range(KC):
            c0 = k * SEG + tile * 512
            bd.emit("vector", OP("tensor_mul", t16[:], x_mid[:, c0:c0 + 512],
                                 A6[:]),
                    waits=[(s_pe, nab), vch], inc=vch)
            bd.emit("vector", OP("tensor_add", t26[:], t16[:], B6[:]),
                    waits=[vch], inc=vch)
            bd.emit("vector", OP("tensor_scalar", h2_bf[:, c0:c0 + 512], t26[:],
                                 cb1[:, k:k + 1], sbv[:, k:k + 1],
                                 ALU.mult, ALU.add),
                    waits=[vch], inc=s_h2)
    n_h2_all = s_h2.n
    n_h2_half = [16, 32]
    ctx_ps6.close()
    ctx6.close()
    ctx56.close()

    ctx7 = ExitStack()
    y_acc = SB("y_acc", [128, KC * SEG], BF16, ctx7)
    tadd = SB("tadd", [128, 512], F32, ctx7)
    xr7 = [SB(f"xr7{i}", [128, 512], F32, ctx7) for i in range(2)]
    oscr = [SB(f"oscr{i}", [128, 512], F32, ctx7) for i in range(2)]

    ctx_rt = ExitStack()
    crow = SB("crow", [1, NE * SEG], BF16, ctx_rt)
    lgrow = SB("lgrow", [8, 512], F32, ctx_rt)
    ltm = SB("ltm", [128, 8], F32, ctx_rt)
    rmax = SB("rmax", [128, 1], F32, ctx_rt)
    exv = SB("exv", [128, 8], F32, ctx_rt)
    sumv = SB("sumv", [128, 1], F32, ctx_rt)
    rsum = SB("rsum", [128, 1], F32, ctx_rt)
    scor = SB("scor", [128, 8], F32, ctx_rt)
    mx8 = SB("mx8", [128, 8], F32, ctx_rt)
    msk = SB("msk", [128, 8], F32, ctx_rt)
    ctm = SB("ctm", [128, 8], F32, ctx_rt)

    ctx_wv = ExitStack()
    comb = SB("comb", [128, NE * SEG], BF16, ctx_wv)
    gsil = SB("gsil", [128, FC * SEG], BF16, ctx_wv)
    act_w = [SB(f"act_w{i}", [128, FC * SEG], BF16, ctx_wv) for i in range(2)]
    u1s = [SB(f"u1s{i}", [128, 512], BF16, ctx_wv) for i in range(2)]
    dnb = [SB(f"dnb{i}", [128, FC * 128], BF16, ctx_wv) for i in range(4)]

    ctx_ps7 = ExitStack()
    y_ps = PSUM("y_ps", [128, 2048], F32, ctx_ps7)
    gu_ps = [PSUM(f"gu{i}", [128, 512], F32, ctx_ps7) for i in range(2)]
    rt_ps = PSUM("rt_ps", [8, 512], F32, ctx_ps7)
    lt_ps = PSUM("lt_ps", [128, 8], F32, ctx_ps7)

    s_gs = bd.sem()
    s_act = bd.sem()
    s_ye = bd.sem()
    s_xr7 = [bd.sem() for _ in range(2)]
    s_oev = bd.sem()
    s_sto = [bd.sem() for _ in range(2)]

    bd.inject_all((s_spl, 16))

    comb_tok = [0]
    def emit_routing():
        rt_free = 0
        lt_free = 0
        cr_free = 0
        for half in range(2):
            def lgm(e, half=half, nrt=max(rt_free, cr_free)):
                e.wait_ge(s_h2.h, n_h2_half[half])
                e.wait_ge(s_gin.h, n_gin1)
                if nrt:
                    e.wait_ge(vch.h, nrt)
                ins = None
                for k in range(KC):
                    ins = e.matmul(rt_ps[:], gw_sb[:, k * 8:(k + 1) * 8],
                                   h2_bf[:, k * SEG + half * 512:
                                         k * SEG + half * 512 + 512],
                                   start=(k == 0), stop=(k == KC - 1))
                return ins
            bd.emit("tensor", lgm, inc=s_pe)
            bd.emit("vector", OP("tensor_copy", lgrow[:], rt_ps[:]),
                    waits=[(s_pe, s_pe.n), vch], inc=vch)
            rt_free = vch.n
            n_lgrow = vch.n
            for sub in range(4):
                def ltr(e, sub=sub, nlg=n_lgrow, nlt=lt_free):
                    e.wait_ge(vch.h, max(nlg, nlt))
                    return e.transpose(lt_ps[:], lgrow[:, sub * 128:(sub + 1) * 128],
                                       identf[0:8, 0:8])
                bd.emit("tensor", ltr, waits=[(s_in, n_in1)], inc=s_pe)
                bd.emit("vector", OP("tensor_copy", ltm[:], lt_ps[:]),
                        waits=[(s_pe, s_pe.n), vch], inc=vch)
                lt_free = vch.n
                bd.emit("vector", OP("reduce_max", rmax[:], ltm[:], AX.X),
                        waits=[vch], inc=vch)
                bd.emit("vector", OP("tensor_scalar_mul", rmax[:], rmax[:], -1.0),
                        waits=[vch], inc=vch)
                bd.emit("scalar", OP("activation", exv[:], ltm[:], AF.Exp,
                                     bias=rmax[:, 0:1]),
                        waits=[vch, sch], inc=sch)
                bd.emit("vector", OP("reduce_sum", sumv[:], exv[:], AX.X),
                        waits=[sch, vch], inc=vch)
                bd.emit("vector", OP("reciprocal", rsum[:], sumv[:]),
                        waits=[vch], inc=vch)
                bd.emit("vector", OP("tensor_scalar_mul", scor[:], exv[:],
                                     rsum[:, 0:1]), waits=[vch], inc=vch)
                bd.emit("vector", OP("max", out=mx8[:], in_=scor[:]),
                        waits=[vch], inc=vch)
                bd.emit("vector", OP("tensor_scalar", msk[:], scor[:],
                                     mx8[:, 1:2], None, ALU.is_ge),
                        waits=[vch], inc=vch)
                bd.emit("vector", OP("tensor_mul", ctm[:], scor[:], msk[:]),
                        waits=[vch], inc=vch)
                n_ctm = vch.n
                for e2 in range(NE):
                    def ctr(e, e2=e2, nctm=n_ctm, ncr=cr_free):
                        e.wait_ge(vch.h, max(nctm, ncr))
                        return e.transpose(rt_ps[0:1, 0:128], ctm[:, e2:e2 + 1],
                                           identf[:])
                    bd.emit("tensor", ctr, inc=s_pe)
                    c0 = e2 * SEG + half * 512 + sub * 128
                    bd.emit("vector", OP("tensor_copy", crow[:, c0:c0 + 128],
                                         rt_ps[0:1, 0:128]),
                            waits=[(s_pe, s_pe.n), vch], inc=vch)
                    cr_free = vch.n
        n_crow = vch.n
        cb_free = 0
        for e2 in range(NE):
            for half in range(2):
                c0 = e2 * SEG + half * 512

                def cbm(e, c0=c0, ncb=cb_free):
                    e.wait_ge(vch.h, n_crow)
                    if ncb:
                        e.wait_ge(vch.h, ncb)
                    return e.matmul(y_ps[:, 0:512], ones1b[:], crow[:, c0:c0 + 512],
                                    start=True, stop=True)
                bd.emit("tensor", cbm, waits=[s_vz], inc=s_pe)
                bd.emit("vector", OP("tensor_copy", comb[:, c0:c0 + 512],
                                     y_ps[:, 0:512]),
                        waits=[(s_pe, s_pe.n), vch], inc=vch)
                cb_free = vch.n
            comb_tok[0] = vch.n


    emit_routing()

    gu_free = [None, None]
    act_ready = {}
    gs_ready = {}
    gs_cons = {}
    y_bank = {}
    wb7_used = list(wb_used)
    db_used = [0] * 4
    for w in range(4):
        e_pair = (2 * w, 2 * w + 1)
        for ex in e_pair:
            ai = ex % 2
            for ms in range(GUC):
                bi = ms % 2
                bd.emit("gpsimd", OP("dma_start",
                                     out=wbuf[bi][:].rearrange("p (c m) -> p c m",
                                                               c=KC),
                                     in_=gup5[ex, ms]),
                        waits=[(s_pe, wb7_used[bi])], inc=(s_wb[bi], 16))
                for half in range(2):
                    i2 = (ms * 2 + half) % 2

                    def gmm(e, bi=bi, half=half, i2=i2, nwb=s_wb[bi].n,
                            ngu=gu_free[i2]):
                        e.wait_ge(s_wb[bi].h, nwb)
                        e.wait_ge(s_h2.h, n_h2_half[half])
                        if ngu:
                            e.wait_ge(ngu[0].h, ngu[1])
                        ins = None
                        for k in range(KC):
                            ins = e.matmul(gu_ps[i2][:],
                                           wbuf[bi][:, k * 128:(k + 1) * 128],
                                           h2_bf[:, k * SEG + half * 512:
                                                 k * SEG + half * 512 + 512],
                                           start=(k == 0), stop=(k == KC - 1))
                        return ins
                    bd.emit("tensor", gmm, inc=s_pe)
                    npe = s_pe.n
                    c0 = (ms % FC) * SEG + half * 512
                    if ms < FC:
                        bd.emit("scalar", OP("activation", gsil[:, c0:c0 + 512],
                                             gu_ps[i2][:], AF.Silu),
                                waits=[(s_pe, npe), sch,
                                       (s_act, gs_cons.get((ms, half), 0))],
                                inc=s_gs)
                        gs_ready[(ex, ms, half)] = s_gs.n
                        gu_free[i2] = (s_gs, s_gs.n)
                    else:
                        fc = ms - FC
                        bd.emit("vector", OP("tensor_mul", u1s[i2][:],
                                             gu_ps[i2][:],
                                             comb[:, ex * SEG + half * 512:
                                                  ex * SEG + half * 512 + 512]),
                                waits=[(s_pe, npe), (vch, comb_tok[0]), vch],
                                inc=s_ev)
                        gu_free[i2] = (s_ev, s_ev.n)
                        bd.emit("vector", OP("tensor_mul",
                                             act_w[ai][:, c0:c0 + 512],
                                             u1s[i2][:], gsil[:, c0:c0 + 512]),
                                waits=[(s_gs, gs_ready[(ex, fc, half)]), vch],
                                inc=s_act)
                        act_ready[(ex, fc, half)] = s_act.n
                        gs_cons[(fc, half)] = s_act.n
                wb7_used[bi] = s_pe.n
        dn_mm_done = {}
        for mh in range(KC):
            dbb = (mh % 2) * 2
            for ei, ex in enumerate(e_pair):
                bd.emit("gpsimd", OP("dma_start",
                                     out=dnb[dbb + ei][:].rearrange(
                                         "p (c m) -> p c m", c=FC),
                                     in_=dn5[ex, mh]),
                        waits=[(s_pe, dn_mm_done.get(mh - 2, db_used[0]))],
                        inc=(s_db[dbb + ei], 16))
            ndb = (s_db[dbb].n, s_db[dbb + 1].n)
            for half in range(2):
                y0 = ((mh % 2) * 2 + half) * 512

                def dmm(e, e_pair=e_pair, half=half, y0=y0, ndb=ndb, dbb=dbb,
                        nacts=max(act_ready[(ex, fc, half)]
                                  for ex in e_pair for fc in range(FC)),
                        nye=y_bank.get((y0 // 512), 0)):
                    e.wait_ge(s_db[dbb].h, ndb[0])
                    e.wait_ge(s_db[dbb + 1].h, ndb[1])
                    e.wait_ge(s_act.h, nacts)
                    if nye:
                        e.wait_ge(s_ye.h, nye)
                    ins = None
                    n = 0
                    for ei2, ex in enumerate(e_pair):
                        for k in range(FC):
                            ins = e.matmul(y_ps[:, y0:y0 + 512],
                                           dnb[dbb + ei2][:, k * 128:(k + 1) * 128],
                                           act_w[ex % 2][:, k * SEG + half * 512:
                                                          k * SEG + half * 512 + 512],
                                           start=(n == 0), stop=(n == 2 * FC - 1))
                            n += 1
                    return ins
                bd.emit("tensor", dmm, inc=s_pe)
                c0 = mh * SEG + half * 512
                if w == 0:
                    bd.emit("vector", OP("tensor_copy", y_acc[:, c0:c0 + 512],
                                         y_ps[:, y0:y0 + 512]),
                            waits=[(s_pe, s_pe.n), vch], inc=s_ye)
                else:
                    bd.emit("vector", OP("tensor_add", y_acc[:, c0:c0 + 512],
                                         y_acc[:, c0:c0 + 512],
                                         y_ps[:, y0:y0 + 512]),
                            waits=[(s_pe, s_pe.n), vch], inc=s_ye)
                y_bank[(y0 // 512)] = s_ye.n
            dn_mm_done[mh] = s_pe.n
        db_used = [s_pe.n] * 4

    bd.flush_block()
    ctx_rt.close()
    ctx_wv.close()

    ctx_sh = ExitStack()
    ssil = SB("ssil", [128, FSC * SEG], BF16, ctx_sh)
    sact = SB("sact", [128, FSC * SEG], BF16, ctx_sh)
    sdnb = [SB(f"sdnb{i}", [128, FSC * 128], BF16, ctx_sh) for i in range(4)]

    sgs_ready = {}
    sact_ready = {}
    for ms in range(SGUC):
        bi = ms % 2
        bd.emit("gpsimd", OP("dma_start",
                             out=wbuf[bi][:].rearrange("p (c m) -> p c m", c=KC),
                             in_=sgu5[ms]),
                waits=[(s_pe, wb7_used[bi])], inc=(s_wb[bi], 16))
        for half in range(2):
            i2 = (ms * 2 + half) % 2

            def sgm(e, bi=bi, half=half, i2=i2, nwb=s_wb[bi].n,
                    ngu=gu_free[i2]):
                e.wait_ge(s_wb[bi].h, nwb)
                e.wait_ge(s_h2.h, n_h2_half[half])
                if ngu:
                    e.wait_ge(ngu[0].h, ngu[1])
                ins = None
                for k in range(KC):
                    ins = e.matmul(gu_ps[i2][:],
                                   wbuf[bi][:, k * 128:(k + 1) * 128],
                                   h2_bf[:, k * SEG + half * 512:
                                         k * SEG + half * 512 + 512],
                                   start=(k == 0), stop=(k == KC - 1))
                return ins
            bd.emit("tensor", sgm, inc=s_pe)
            npe = s_pe.n
            c0 = (ms % FSC) * SEG + half * 512
            if ms < FSC:
                bd.emit("scalar", OP("activation", ssil[:, c0:c0 + 512],
                                     gu_ps[i2][:], AF.Silu),
                        waits=[(s_pe, npe), sch], inc=s_gs)
                sgs_ready[(ms, half)] = s_gs.n
                gu_free[i2] = (s_gs, s_gs.n)
            else:
                fc = ms - FSC
                bd.emit("vector", OP("tensor_mul", sact[:, c0:c0 + 512],
                                     gu_ps[i2][:], ssil[:, c0:c0 + 512]),
                        waits=[(s_pe, npe), (s_gs, sgs_ready[(fc, half)]), vch],
                        inc=s_act)
                sact_ready[(fc, half)] = s_act.n
                gu_free[i2] = (s_act, s_act.n)
        wb7_used[bi] = s_pe.n

    sdn_mm_done = {}
    xr7_free = [0, 0]
    osc_free = [0, 0]
    for mh in range(KC):
        bi = mh % 4
        bd.emit("gpsimd", OP("dma_start",
                             out=sdnb[bi][:].rearrange("p (c m) -> p c m", c=FSC),
                             in_=sdn5[mh]),
                waits=[(s_pe, sdn_mm_done.get(mh - 4, db_used[0]))],
                inc=(s_db[bi], 16))
        for half in range(2):
            y0 = ((mh % 2) * 2 + half) * 512
            i2 = (mh * 2 + half) % 2
            bd.emit("sync", OP("dma_start", out=xr7[i2][:],
                               in_=xmd.ap()[:, mh:mh + 1,
                                            half * 512:half * 512 + 512]
                               .rearrange("p one t -> p (one t)")),
                    waits=[(s_spl, 16), (s_oev, xr7_free[i2])],
                    inc=(s_xr7[i2], 16))
            nxr = s_xr7[i2].n

            def sdm(e, bi=bi, half=half, y0=y0, ndb=s_db[bi].n,
                    nacts=max(sact_ready[(fc, half)] for fc in range(FSC)),
                    nye=y_bank.get((y0 // 512), 0)):
                e.wait_ge(s_db[bi].h, ndb)
                e.wait_ge(s_act.h, nacts)
                if nye:
                    e.wait_ge(s_ye.h, nye)
                ins = None
                for k in range(FSC):
                    ins = e.matmul(y_ps[:, y0:y0 + 512],
                                   sdnb[bi][:, k * 128:(k + 1) * 128],
                                   sact[:, k * SEG + half * 512:
                                        k * SEG + half * 512 + 512],
                                   start=(k == 0), stop=(k == FSC - 1))
                return ins
            bd.emit("tensor", sdm, inc=s_pe)
            c0 = mh * SEG + half * 512
            bd.emit("vector", OP("tensor_add", tadd[:], y_acc[:, c0:c0 + 512],
                                 y_ps[:, y0:y0 + 512]),
                    waits=[(s_pe, s_pe.n), (s_ye, s_ye.n), vch],
                    inc=s_ye)
            y_bank[(y0 // 512)] = s_ye.n
            bd.emit("vector", OP("scalar_tensor_tensor", oscr[i2][:], tadd[:],
                                 gbv[:, mh:mh + 1], xr7[i2][:],
                                 ALU.mult, ALU.add),
                    waits=[(s_xr7[i2], nxr), (s_mv, n_modv),
                           (s_sto[i2], osc_free[i2])],
                    inc=s_oev)
            xr7_free[i2] = s_oev.n
            n_osc = s_oev.n
            bd.emit("sync", OP("dma_start",
                               out=out5[:, mh:mh + 1,
                                        half * 512:half * 512 + 512]
                               .rearrange("p one t -> p (one t)"),
                               in_=oscr[i2][:]),
                    waits=[(s_oev, n_osc)], inc=(s_sto[i2], 16))
            osc_free[i2] = s_sto[i2].n
        sdn_mm_done[mh] = s_pe.n

    n0, n1 = s_sto[0].n, s_sto[1].n

    def fin(e):
        e.wait_ge(s_sto[0].h, n0)
        return e.wait_ge(s_sto[1].h, n1)
    bd.emit("sync", fin)
    bd.flush_block()

    ctx_ps7.close()
    ctx_sh.close()
    ctx7.close()
    ctx67.close()
    gctx.close()
    return nc



def _tables(seg):
    sl = _slopes()
    r = np.exp(-sl)
    t = np.arange(seg * SEG, (seg + 1) * SEG, dtype=np.float64)
    j = np.arange(32, dtype=np.float64)
    freq = (1.0 / 1024.0) ** (j / 31.0)
    th = t[None, :] * freq[:, None]
    ct = np.ones((128, SEG))
    ct[0:32] = np.cos(th)
    ct[64:96] = np.cos(th)
    st = np.zeros((128, SEG))
    st[0:32] = np.sin(th)
    st[64:96] = -np.sin(th)
    ii = np.arange(128, dtype=np.float64)
    diff = ii[None, :] - ii[:, None]
    m = diff >= 0
    dmt = np.zeros((128, NH, 128))
    for h in range(NH):
        dmt[:, h, :] = np.where(m, r[h] ** np.where(m, diff, 0.0), 0.0)
    qd = np.stack([r[h] ** (ii + 1.0) for h in range(NH)], axis=0)
    qdb = np.broadcast_to(qd[None, :, :], (128, NH, 128)).copy()
    kdc = np.stack([r[h] ** (127.0 - ii) for h in range(NH)], axis=1)
    rcb = np.broadcast_to(
        np.repeat(r ** float(CH), 128)[None, :], (128, NH * 128)).copy()
    dsg = np.zeros((128, 3 * NH))
    for s2 in range(3):
        if s2 < seg:
            d = r ** (float(SEG) * (seg - 1 - s2))
            dsg[:, s2 * NH:(s2 + 1) * NH] = d[None, :]
    swp = np.zeros((128, 128))
    for m_ in range(128):
        swp[(m_ + 64) % 128, m_] = 1.0
    bf = lambda a: np.ascontiguousarray(a).astype(ml_dtypes.bfloat16)
    f32 = lambda a: np.ascontiguousarray(a).astype(np.float32)
    return dict(ct=bf(ct), st=bf(st), dmt=bf(dmt), qdb=bf(qdb),
                kdc=f32(kdc), dsg=f32(dsg), rcb=bf(rcb),
                idf=f32(np.eye(128)), idb=bf(np.eye(128)), swb=bf(swp))


def _retile_w(w, n_ms, m_w):
    K, M = w.shape
    kc = K // 128
    return np.ascontiguousarray(
        w.reshape(kc, 128, n_ms, m_w).transpose(2, 1, 0, 3)).astype(np.float32)


_PROG = []


def kernel(L0_attn_x, L0_attn_Wq, L0_attn_Wk, L0_attn_Wv, L0_attn_Wo,
           L0_attn_rms_w, L0_ffn_gate_weight, L0_ffn_gate_up_weights,
           L0_ffn_down_weights, L0_ffn_shared_gate_up, L0_ffn_shared_down,
           L0_c, L0_wmod):
    x = np.asarray(L0_attn_x, dtype=np.float32)
    Wq = np.asarray(L0_attn_Wq, dtype=np.float32)
    Wk = np.asarray(L0_attn_Wk, dtype=np.float32)
    Wv = np.asarray(L0_attn_Wv, dtype=np.float32)
    Wo = np.asarray(L0_attn_Wo, dtype=np.float32)
    rms_w = np.asarray(L0_attn_rms_w, dtype=np.float32)
    gw = np.asarray(L0_ffn_gate_weight, dtype=np.float32)
    gup = np.asarray(L0_ffn_gate_up_weights, dtype=np.float32)
    dn = np.asarray(L0_ffn_down_weights, dtype=np.float32)
    sgu = np.asarray(L0_ffn_shared_gate_up, dtype=np.float32)
    sdn = np.asarray(L0_ffn_shared_down, dtype=np.float32)
    c = np.asarray(L0_c, dtype=np.float32)
    wmod = np.asarray(L0_wmod, dtype=np.float32)

    wq5 = _retile_w(Wq, KC, 128)
    wk5 = _retile_w(Wk, KC, 128)
    wo5 = _retile_w(Wo, KC, 128)
    wv5 = _retile_w(Wv, 4, 512)
    gup5 = np.stack([_retile_w(gup[e].T, GUC, 128) for e in range(NE)])
    dn5 = np.stack([_retile_w(dn[e].T, KC, 128) for e in range(NE)])
    sgu5 = _retile_w(sgu.T, SGUC, 128)
    sdn5 = _retile_w(sdn.T, KC, 128)
    wmod5 = _retile_w(wmod, 24, 512)
    gw5 = np.ascontiguousarray(gw.T.reshape(KC, 128, 8).transpose(1, 0, 2)
                               ).astype(np.float32)
    rmsw5 = np.ascontiguousarray(rms_w.reshape(KC, 128).T).astype(np.float32)

    if not _PROG:
        _PROG.append(build_program())
    nc = _PROG[0]

    in_maps = []
    for core in range(NCORE):
        b, seg = core // 4, core % 4
        xs = x[b, seg * SEG:(seg + 1) * SEG, :]
        x5 = np.ascontiguousarray(xs.T.reshape(KC, 128, SEG).transpose(1, 0, 2)
                                  ).astype(np.float32)
        c_kb = np.ascontiguousarray(c[b].reshape(KC, 128).T).astype(np.float32)
        tb = _tables(seg)
        in_maps.append(dict(
            x5=x5, wq5=wq5, wk5=wk5, wo5=wo5, wv5=wv5, gup5=gup5, dn5=dn5,
            sgu5=sgu5, sdn5=sdn5, wmq=np.ascontiguousarray(
                wmod5[seg * 6:(seg + 1) * 6]),
            gw5=gw5, c_kb=c_kb, rmsw=rmsw5, **tb))

    res = run_bass_kernel_spmd(nc, in_maps, core_ids=list(range(NCORE)))

    out = np.empty((2, 4096, E), dtype=np.float32)
    for core in range(NCORE):
        b, seg = core // 4, core % 4
        o5 = np.asarray(res.results[core]["out5"])
        out[b, seg * SEG:(seg + 1) * SEG, :] = (
            o5.transpose(1, 0, 2).reshape(E, SEG).T)
    return out


# version 12 (final)
# speedup vs baseline: 1.0102x; 1.0102x over previous
import math
from contextlib import ExitStack

import numpy as np
import ml_dtypes

import concourse.bass as bass
import concourse.mybir as mybir
from concourse.bass_utils import run_bass_kernel_spmd

F32 = mybir.dt.float32
BF16 = mybir.dt.bfloat16
AF = mybir.ActivationFunctionType
ALU = mybir.AluOpType
AX = mybir.AxisListType

E = 2048
NH = 16
HD = 128
CH = 128
NCORE = 8
SEG = 1024
NCHK = SEG // CH
KC = E // 128
NE = 8
F = 1408
FC = F // 128
GUC = 2 * FC
FS = 2816
FSC = FS // 128
SGUC = 2 * FSC
JW = 12288
EPS = 1e-5

ENGS = ("sync", "gpsimd", "tensor", "vector", "scalar")


def _slopes():
    start = 2.0 ** (-(2.0 ** -(math.log2(NH) - 3)))
    return np.array([start * start ** i for i in range(NH)], dtype=np.float64)


class Sem:
    __slots__ = ("h", "n")


class Bld:
    def __init__(self, nc, ctx):
        self.nc = nc
        self.ctx = ctx
        self.ops = {e: [] for e in ENGS}
        self.last_wait = {e: {} for e in ENGS}
        self.prev_tok = {}
        self.pending = {}
        self._sid = 0

    def sem(self):
        s = Sem()
        self._sid += 1
        s.h = self.ctx.enter_context(self.nc.semaphore(f"s{self._sid}"))
        s.n = 0
        return s

    def inject_all(self, tok):
        for e in ENGS:
            self.pending.setdefault(e, []).append(tok)

    def emit(self, eng, build, waits=(), inc=None):
        wl = []
        allw = list(waits)
        if self.pending.get(eng):
            allw.extend(self.pending.pop(eng))
        if eng in ("vector", "scalar") and self.prev_tok.get(eng):
            allw.append(self.prev_tok[eng])
        for w in allw:
            s, v = w if isinstance(w, tuple) else (w, w.n)
            if v > 0 and self.last_wait[eng].get(id(s), -1) < v:
                wl.append((s.h, v))
                self.last_wait[eng][id(s)] = v
        il = []
        if inc is not None:
            s, k = inc if isinstance(inc, tuple) else (inc, 1)
            s.n += k
            il.append((s.h, k))
            if eng in ("vector", "scalar"):
                self.prev_tok[eng] = (s, s.n)

        def fn(e, wl=wl, il=il, build=build):
            for h, v in wl:
                e.wait_ge(h, v)
            ins = build(e)
            for h, k in il:
                ins.then_inc(h, k)
        self.ops[eng].append(fn)

    def flush_block(self):
        ops = self.ops
        self.ops = {e: [] for e in ENGS}
        with self.nc.Block() as b:
            regs = {"sync": b.sync, "gpsimd": b.gpsimd, "tensor": b.tensor,
                    "vector": b.vector, "scalar": b.scalar}
            for name in ENGS:
                if ops[name]:
                    regs[name](lambda e, l=ops[name]: [f(e) for f in l])


def OP(name, *a, **k):
    return lambda e: getattr(e, name)(*a, **k)


class Arena:

    def __init__(self, nc, ctx, size=212832):
        self.nc = nc
        self.size = size
        self.backing = ctx.enter_context(
            nc.sbuf_tensor("arena", [128, size], mybir.dt.uint8))
        self.base = nc.lookup_mloc(self.backing).addr
        self.free_list = [(0, size)]
        self.live = {}

    def alloc(self, name, shape, dtype):
        per = 1
        for s in shape[1:]:
            per *= s
        per *= mybir.dt.size(dtype)
        per = (per + 31) & ~31
        for idx, (off, ln) in enumerate(self.free_list):
            if ln >= per:
                break
        else:
            live = sorted(self.live.items(), key=lambda kv: -kv[1])
            raise MemoryError(
                f"arena OOM for {name} ({per}B); live: " +
                ", ".join(f"{k}={v}" for k, v in live))
        self.free_list[idx] = (off + per, ln - per)
        if self.free_list[idx][1] == 0:
            del self.free_list[idx]
        hnd = self.nc.alloc_sbuf_tensor_at(name, shape, dtype,
                                           offset=self.base + off)
        self.live[name] = per
        return hnd, off, per

    def free(self, off, per, name=None):
        if name:
            self.live.pop(name, None)
        self.free_list.append((off, per))
        self.free_list.sort()
        merged = []
        for o, l in self.free_list:
            if merged and merged[-1][0] + merged[-1][1] == o:
                merged[-1] = (merged[-1][0], merged[-1][1] + l)
            else:
                merged.append((o, l))
        self.free_list = [tuple(m) for m in merged]

    def peak_used(self):
        free_now = sum(l for _, l in self.free_list)
        return self.size - free_now


def build_program():
    sl = _slopes()
    r = np.exp(-sl)
    rC = (r ** float(CH)).tolist()

    nc = bass.Bass("TRN2", target_bir_lowering=False, debug=False)

    DI = lambda n, s, dt=F32: nc.dram_tensor(n, s, dt, kind="ExternalInput").ap()
    x5 = DI("x5", [128, KC, SEG])
    wq5 = DI("wq5", [KC, 128, KC, 128])
    wk5 = DI("wk5", [KC, 128, KC, 128])
    wo5 = DI("wo5", [KC, 128, KC, 128])
    wv5 = DI("wv5", [4, 128, KC, 512])
    gup5 = DI("gup5", [NE, GUC, 128, KC, 128])
    dn5 = DI("dn5", [NE, KC, 128, FC, 128])
    sgu5 = DI("sgu5", [SGUC, 128, KC, 128])
    sdn5 = DI("sdn5", [KC, 128, FSC, 128])
    gw5 = DI("gw5", [128, KC, 8])
    c_kb5 = DI("c_kb", [128, KC])
    rmsw5 = DI("rmsw", [128, KC])
    ct5 = DI("ct", [128, SEG], BF16)
    st5 = DI("st", [128, SEG], BF16)
    dmt5 = DI("dmt", [128, NH, 128], BF16)
    qdb5 = DI("qdb", [128, NH, 128], BF16)
    kdc5 = DI("kdc", [128, NH])
    dsg5 = DI("dsg", [128, 3 * NH])
    rcb5 = DI("rcb", [128, NH * 128], BF16)
    idf5 = DI("idf", [128, 128])
    idb5 = DI("idb", [128, 128], BF16)
    swb5 = DI("swb", [128, 128], BF16)
    wmq5 = DI("wmq", [6, 128, KC, 512])

    out5 = nc.dram_tensor("out5", [128, KC, SEG], F32, kind="ExternalOutput").ap()

    agi = nc.dram_tensor("agi", [128, NH * HD], F32)
    ago = nc.dram_tensor("ago", [4 * 128, NH * HD], F32)
    mgi = nc.dram_tensor("mgi", [1, JW // 4], F32)
    mgo = nc.dram_tensor("mgo", [4, JW // 4], F32)
    xmd = nc.dram_tensor("xmd", [128, KC, SEG], F32)

    gctx = ExitStack()
    bd = Bld(nc, gctx)
    arena = Arena(nc, gctx)

    def SB(n, s, dt=F32, ctx=gctx):
        hnd, off, per = arena.alloc(n, s, dt)
        ctx.callback(arena.free, off, per, n)
        return hnd

    def PSUM(n, s, dt=F32, ctx=gctx):
        return ctx.enter_context(nc.psum_tensor(n, s, dt))

    identf = SB("identf", [128, 128])
    identb = SB("identb", [128, 128], BF16)
    swapb = SB("swapb", [128, 128], BF16)
    kdec = SB("kdec", [128, NH])
    dsg = SB("dsg_s", [128, 3 * NH])
    rmsw = SB("rmsw_s", [128, KC])
    c_kb = SB("ckb_s", [128, KC])
    s_kbt = SB("s_kbt", [128, KC], BF16)
    gw_sb = SB("gw_sb", [128, KC * 8], BF16)
    onesf = SB("onesf", [1, 128])
    onesb = SB("onesb", [128, 1], BF16)
    ones1b = SB("ones1b", [1, 128], BF16)
    ca1 = SB("ca1", [128, KC])
    sav = SB("sav", [128, KC])
    gav = SB("gav", [128, KC])
    cb1 = SB("cb1", [128, KC])
    sbv = SB("sbv", [128, KC])
    gbv = SB("gbv", [128, KC])
    gscr = SB("gscr", [128, KC])
    sgn = SB("sgn", [128, KC])
    wbuf = [SB(f"wbuf{i}", [128, KC * 128], BF16) for i in range(2)]

    def mk_rows(ctx, tag):
        d = {}
        for nm in ("t1", "t2", "sig", "n1"):
            d[nm] = SB(f"r_{nm}{tag}", [1, 512], F32, ctx)
        for nm in ("mu", "is_", "b"):
            d[nm] = [SB(f"r_{nm}{tag}{i}", [1, 512], F32, ctx) for i in range(2)]
        return d

    s_in = bd.sem()
    s_gin = bd.sem()
    s_vz = bd.sem()
    vch = bd.sem()
    sch = bd.sem()
    s_st = bd.sem()
    s_wb = [bd.sem() for _ in range(2)]
    s_vb = [bd.sem() for _ in range(2)]
    s_db = [bd.sem() for _ in range(4)]
    s_pe = bd.sem()
    s_ev = bd.sem()
    s_evs = bd.sem()
    s_evu = bd.sem()
    s_sq = bd.sem()
    s_row = bd.sem()
    s_h = bd.sem()

    def rowmath(R, ssum_ps, ssq_ps, tile, w_pe, n, rms_only=False):
        o_mu = None if rms_only else R["mu"][tile]
        o_is = R["is_"][tile]
        w0 = [(s_pe, w_pe)]
        if not rms_only:
            bd.emit("vector", OP("tensor_scalar_mul", o_mu[:], ssum_ps[:], 1.0 / n),
                    waits=w0 + [vch], inc=vch)
        bd.emit("vector", OP("tensor_scalar_mul", R["t1"][:], ssq_ps[:], 1.0 / n),
                waits=w0 + [vch], inc=vch)
        if not rms_only:
            bd.emit("vector", OP("tensor_mul", R["t2"][:], o_mu[:], o_mu[:]),
                    waits=[vch], inc=vch)
            bd.emit("vector", OP("tensor_sub", R["t1"][:], R["t1"][:], R["t2"][:]),
                    waits=[vch], inc=vch)
        bd.emit("vector", OP("tensor_scalar_add", R["t2"][:], R["t1"][:], EPS),
                waits=[vch], inc=vch)
        bd.emit("scalar", OP("activation", R["sig"][:], R["t2"][:], AF.Sqrt),
                waits=[vch, sch], inc=sch)
        bd.emit("vector", OP("reciprocal", o_is[:], R["sig"][:]),
                waits=[sch], inc=vch)
        bd.emit("vector", OP("tensor_mul", R["n1"][:], o_is[:], o_is[:]),
                waits=[vch], inc=vch)
        bd.emit("vector", OP("tensor_mul", R["n1"][:], R["n1"][:], R["t2"][:]),
                waits=[vch], inc=vch)
        bd.emit("vector", OP("tensor_scalar", R["n1"][:], R["n1"][:], -0.5, 1.5,
                             ALU.mult, ALU.add), waits=[vch], inc=vch)
        if rms_only:
            bd.emit("vector", OP("tensor_mul", o_is[:], o_is[:], R["n1"][:]),
                    waits=[vch], inc=s_row)
        else:
            bd.emit("vector", OP("tensor_mul", o_is[:], o_is[:], R["n1"][:]),
                    waits=[vch], inc=vch)
            bd.emit("vector", OP("scalar_tensor_tensor", R["b"][tile][:], o_mu[:],
                                 -1.0, o_is[:], ALU.mult, ALU.mult),
                    waits=[vch], inc=s_row)

    ctx12 = ExitStack()
    ctx1m = ExitStack()
    ctx_rot = ExitStack()
    x_bf = SB("x_bf", [128, KC * SEG], BF16, ctx12)
    ct_sb = SB("ct_sb", [128, SEG], BF16, ctx_rot)
    st_sb = SB("st_sb", [128, SEG], BF16, ctx_rot)
    ctx_bb = ExitStack()
    bigbuf = [SB(f"bigbuf{i}", [128, KC * 512], BF16, ctx_bb) for i in range(2)]
    xsq = [SB(f"xsq{i}", [128, 512], BF16, ctx12) for i in range(2)]
    RA = mk_rows(ctx12, "a")
    mrowf = SB("mrowf", [1, JW // 4], F32, ctx1m)
    mvr = SB("mvr", [16, 6 * 128], F32, ctx1m)

    ctx_ps1 = ExitStack()
    mps = [PSUM(f"mps{i}", [1, 512], F32, ctx_ps1) for i in range(2)]
    mtr = PSUM("mtr", [128, 16], F32, ctx_ps1)
    ssum = PSUM("ssum", [1, 512], F32, ctx_ps1)
    ssq = PSUM("ssq", [1, 512], F32, ctx_ps1)

    for dst, src in ((identf, idf5), (kdec, kdc5), (dsg, dsg5), (rmsw, rmsw5),
                     (c_kb, c_kb5), (identb, idb5), (swapb, swb5),
                     (ct_sb, ct5), (st_sb, st5)):
        bd.emit("sync", OP("dma_start", out=dst[:], in_=src), inc=(s_in, 16))
    n_in1 = s_in.n
    bd.emit("gpsimd", OP("dma_start",
                         out=x_bf[:].rearrange("p (c t) -> p c t", c=KC), in_=x5),
            inc=(s_gin, 16))
    bd.emit("gpsimd", OP("dma_start",
                         out=gw_sb[:].rearrange("p (c m) -> p c m", c=KC), in_=gw5),
            inc=(s_gin, 16))
    n_gin1 = s_gin.n

    bd.emit("vector", OP("memset", onesf[:], 1.0), inc=s_vz)
    bd.emit("vector", OP("memset", onesb[:], 1.0), inc=s_vz)
    bd.emit("vector", OP("memset", ones1b[:], 1.0), inc=s_vz)

    bd.emit("scalar", OP("activation", s_kbt[:], c_kb[:], AF.Silu),
            waits=[(s_in, n_in1), sch], inc=sch)
    n_silu_c = sch.n

    mv_used = [0, 0]
    mev = [0, 0]
    s_mcc = bd.sem()
    for js in range(6):
        bi = js % 2
        bd.emit("gpsimd", OP("dma_start",
                             out=bigbuf[bi][:].rearrange("p (c j) -> p c j", c=KC),
                             in_=wmq5[js]),
                waits=[(s_pe, mv_used[bi])], inc=(s_vb[bi], 16))

        def mv(e, bi=bi, nvb=s_vb[bi].n, nev=mev[bi], nsch=n_silu_c):
            e.wait_ge(s_vb[bi].h, nvb)
            e.wait_ge(sch.h, nsch)
            if nev:
                e.wait_ge(s_ev.h, nev)
            ins = None
            for k in range(KC):
                ins = e.matmul(mps[bi][:], s_kbt[:, k:k + 1],
                               bigbuf[bi][:, k * 512:(k + 1) * 512],
                               start=(k == 0), stop=(k == KC - 1))
            return ins
        bd.emit("tensor", mv, inc=s_pe)
        mv_used[bi] = s_pe.n
        bd.emit("vector", OP("tensor_copy", mrowf[:, js * 512:(js + 1) * 512],
                             mps[bi][:]),
                waits=[(s_pe, s_pe.n), vch], inc=s_ev)
        mev[bi] = s_ev.n
    n_mrow = s_ev.n

    s_md = bd.sem()
    bd.emit("sync", OP("dma_start", out=mgi.ap(), in_=mrowf[:, 0:JW // 4]),
            waits=[(s_ev, n_mrow)], inc=(s_md, 16))

    def mcoll(e):
        return e.collective_compute(
            "AllGather", ALU.bypass,
            replica_groups=[[0, 1, 2, 3], [4, 5, 6, 7]],
            ins=[mgi.ap().opt()], outs=[mgo.ap().opt()])
    bd.emit("gpsimd", mcoll, waits=[(s_md, 16)], inc=s_mcc)
    bd.emit("sync", OP("dma_start",
                       out=mvr[:].rearrange("c (v p) -> c v p", v=6),
                       in_=mgo.ap().rearrange("q r -> (q r)").rearrange(
                           "(v c p) -> c v p", v=6, c=16)),
            waits=[(s_mcc, 1)], inc=(s_md, 16))

    s_mv = bd.sem()
    for vi, nm in enumerate(["sa", "ca", "ga", "sb", "cb", "gb"]):
        bd.emit("tensor", OP("transpose", mtr[:], mvr[:, vi * 128:(vi + 1) * 128],
                             identf[0:16, 0:16]),
                waits=[(s_md, 32), (s_in, n_in1), (s_mv, vi)], inc=s_pe)
        w = [(s_pe, s_pe.n)]
        if nm in ("sa", "sb"):
            dst = sav if nm == "sa" else sbv
            bd.emit("vector", OP("tensor_copy", dst[:], mtr[:]),
                    waits=w + [vch], inc=s_mv)
        elif nm in ("ca", "cb"):
            dst = ca1 if nm == "ca" else cb1
            bd.emit("vector", OP("tensor_scalar_add", dst[:], mtr[:], 1.0),
                    waits=w + [vch], inc=s_mv)
        else:
            dst = gav if nm == "ga" else gbv
            bd.emit("vector", OP("tensor_copy", gscr[:], mtr[:]),
                    waits=w + [vch], inc=vch)
            bd.emit("vector", OP("tensor_scalar", sgn[:], gscr[:], 0.0, None,
                                 ALU.is_ge), waits=[vch], inc=vch)
            bd.emit("vector", OP("tensor_scalar", sgn[:], sgn[:], 2.0, -1.0,
                                 ALU.mult, ALU.add), waits=[vch], inc=vch)
            bd.emit("scalar", OP("activation", dst[:], gscr[:], AF.Abs),
                    waits=[vch, sch], inc=sch)
            bd.emit("vector", OP("tensor_scalar_max", dst[:], dst[:], 0.25),
                    waits=[sch, vch], inc=vch)
            bd.emit("vector", OP("tensor_mul", dst[:], dst[:], sgn[:]),
                    waits=[vch], inc=s_mv)
    n_modv = s_mv.n

    pair_tok = {}
    for tile in range(2):
        for k in range(KC):
            bi = k % 2
            c0 = k * SEG + tile * 512
            bd.emit("scalar", OP("activation", xsq[bi][:],
                                 x_bf[:, c0:c0 + 512], AF.Square),
                    waits=[(s_gin, n_gin1),
                           (s_pe, pair_tok.get((tile, k - 2),
                                               pair_tok.get((tile - 1, KC - 1), 0)))],
                    inc=s_sq)
            nsq = s_sq.n

            def statk(e, k=k, c0=c0, nsq=nsq, tile=tile):
                e.wait_ge(s_sq.h, nsq)
                e.matmul(ssum[:], onesb[:], x_bf[:, c0:c0 + 512],
                         start=(k == 0), stop=(k == KC - 1))
                return e.matmul(ssq[:], onesb[:], xsq[k % 2][:],
                                start=(k == 0), stop=(k == KC - 1))
            w = [(s_gin, n_gin1), s_vz]
            if k == 0:
                w.append((s_row, s_row.n))
            bd.emit("tensor", statk, waits=w, inc=s_pe)
            pair_tok[(tile, k)] = s_pe.n
        rowmath(RA, ssum, ssq, tile, s_pe.n, E)

    bd.flush_block()
    ctx_ps1.close()
    ctx1m.close()

    ctx_ps2 = ExitStack()
    pq = [PSUM(f"pq{i}", [128, 512], F32, ctx_ps2) for i in range(2)]
    qsw = [PSUM(f"qsw{i}", [128, 512], F32, ctx_ps2) for i in range(2)]
    ctx2s = ExitStack()
    A_ps = PSUM("A_ps", [128, 512], F32, ctx2s)
    B_ps = PSUM("B_ps", [128, 512], F32, ctx2s)
    ctx23 = ExitStack()
    ctx24 = ExitStack()
    h_bf = SB("h_bf", [128, KC * SEG], BF16, ctx23)
    k_sb = SB("k_sb", [128, NH * SEG], BF16, ctx24)
    v_sb = SB("v_sb", [128, NCHK * E], BF16, ctx24)
    t1s = SB("t1s", [128, 512], F32, ctx23)
    t2s = SB("t2s", [128, 512], F32, ctx23)
    qraw = [SB(f"qraw{i}", [128, 512], BF16, ctx23) for i in range(2)]
    wrot = [SB(f"wrot{i}", [128, 512], BF16, ctx23) for i in range(2)]

    for tile in range(2):
        bd.emit("tensor", OP("matmul", A_ps[:], onesf[:], RA["is_"][tile][:],
                             start=True, stop=True),
                waits=[(s_row, tile + 1), (s_h, s_h.n), (s_in, n_in1)],
                inc=s_pe)
        bd.emit("tensor", OP("matmul", B_ps[:], onesf[:], RA["b"][tile][:],
                             start=True, stop=True), inc=s_pe)
        nab = s_pe.n
        for k in range(KC):
            c0 = k * SEG + tile * 512
            bd.emit("vector", OP("tensor_mul", t1s[:], x_bf[:, c0:c0 + 512],
                                 A_ps[:]),
                    waits=[(s_pe, nab), (s_mv, n_modv), vch], inc=vch)
            bd.emit("vector", OP("tensor_add", t2s[:], t1s[:], B_ps[:]),
                    waits=[vch], inc=vch)
            bd.emit("vector", OP("tensor_scalar", h_bf[:, c0:c0 + 512], t2s[:],
                                 ca1[:, k:k + 1], sav[:, k:k + 1],
                                 ALU.mult, ALU.add),
                    waits=[vch], inc=s_h)
    n_h_all = s_h.n
    n_h_tile = [16, 32]
    ctx2s.close()
    ctx12.close()

    ctx34 = ExitStack()
    kvsv = SB("kvsv", [128, NCHK * NH * HD], BF16, ctx34)
    kvf = SB("kvf", [128, NH * HD], F32, ctx34)
    kvin = SB("kvin", [128, NH * HD], F32, ctx34)
    ctx_u7 = ExitStack()
    u7 = SB("u7", [128, NH * HD], BF16, ctx_u7)
    ctx3o = ExitStack()
    ktd = [SB(f"ktd{i}", [128, 128], BF16, ctx3o) for i in range(4)]

    ctx_ps3 = ExitStack()
    tp_ps = [PSUM(f"tp{i}", [128, 128], BF16, ctx_ps3) for i in range(2)]
    U_ps = [PSUM(f"U{i}", [128, 128], F32, ctx_ps3) for i in range(2)]

    s_ag = bd.sem()
    s_cc = bd.sem()
    s_lp = bd.sem()
    s_kvin = bd.sem()

    pq_free = [0, 0]
    qsw_free = [None, None]
    qraw_free = [0, 0]
    rot_ready = {}
    wb_used = [0, 0]

    def proj_rot(w5, dst_sb, dst_name, skip_dma=0):
        for ms in range(16):
            bi = ms % 2
            if ms >= skip_dma:
                bd.emit("gpsimd", OP("dma_start",
                                     out=wbuf[bi][:].rearrange("p (c m) -> p c m", c=KC),
                                     in_=w5[ms]),
                        waits=[(s_pe, wb_used[bi])], inc=(s_wb[bi], 16))
            for tile in range(2):
                i2 = (ms * 2 + tile) % 2
                c0 = ms * SEG + tile * 512

                def mmg(e, bi=bi, tile=tile, i2=i2, nwb=s_wb[bi].n,
                        nev=pq_free[i2]):
                    e.wait_ge(s_wb[bi].h, nwb)
                    e.wait_ge(s_h.h, n_h_tile[tile])
                    if nev:
                        e.wait_ge(s_evs.h, nev)
                    ins = None
                    for k in range(KC):
                        ins = e.matmul(pq[i2][:],
                                       wbuf[bi][:, k * 128:(k + 1) * 128],
                                       h_bf[:, k * SEG + tile * 512:
                                            k * SEG + tile * 512 + 512],
                                       start=(k == 0), stop=(k == KC - 1))
                    return ins
                bd.emit("tensor", mmg, inc=s_pe)
                n_mm = s_pe.n
                bd.emit("scalar", OP("copy", qraw[i2][:], pq[i2][:]),
                        waits=[(s_pe, n_mm), (s_ev, qraw_free[i2])], inc=s_evs)
                pq_free[i2] = s_evs.n
                n_evac = s_evs.n

                def swm(e, i2=i2, nev=n_evac, nfree=qsw_free[i2]):
                    e.wait_ge(s_evs.h, nev)
                    if nfree:
                        e.wait_ge(nfree[0].h, nfree[1])
                    return e.matmul(qsw[i2][:], swapb[:], qraw[i2][:],
                                    start=True, stop=True)
                bd.emit("tensor", swm, waits=[(s_in, n_in1)], inc=s_pe)
                n_sw = s_pe.n
                bd.emit("vector", OP("tensor_mul", wrot[i2][:], qsw[i2][:],
                                     st_sb[:, tile * 512:tile * 512 + 512]),
                        waits=[(s_pe, n_sw), vch], inc=vch)
                qsw_free[i2] = (vch, vch.n)
                bd.emit("vector", OP("tensor_mul", qraw[i2][:], qraw[i2][:],
                                     ct_sb[:, tile * 512:tile * 512 + 512]),
                        waits=[(s_evs, n_evac)], inc=vch)
                bd.emit("vector", OP("tensor_add", dst_sb[:, c0:c0 + 512],
                                     qraw[i2][:], wrot[i2][:]),
                        waits=[vch], inc=s_ev)
                rot_ready[(dst_name, ms, tile)] = s_ev.n
                qraw_free[i2] = s_ev.n
            wb_used[bi] = s_pe.n

    proj_rot(wk5, k_sb, "k")

    vb_used = [0, 0]
    pv_free = [0, 0]
    v_tok = {}
    pa_tok = {}
    tp_free = [0, 0]
    ktd_free = [0, 0, 0, 0]
    u_free = [0, 0, 0, 0]
    pai = [0]
    kvsv0_tok = [0]

    def u_bank(i4):
        return U_ps[i4][:] if i4 < 2 else qsw[i4 - 2][:, 0:128]

    def phase_a(es, tch):
        ktoks = []
        base = pai[0]
        for hh in range(4):
            h = es * 4 + hh
            i2 = (base + hh) % 2
            i4 = (base + hh) % 4
            ks = h * SEG + tch * 128

            def tpm(e, ks=ks, i2=i2, nk=rot_ready[("k", h, tch // 4)],
                    nfree=tp_free[i2]):
                e.wait_ge(s_ev.h, nk)
                if nfree:
                    e.wait_ge(s_evu.h, nfree)
                return e.transpose(tp_ps[i2][:], k_sb[:, ks:ks + 128], identb[:])
            bd.emit("tensor", tpm, inc=s_pe)
            bd.emit("scalar", OP("activation", ktd[i4][:], tp_ps[i2][:],
                                 AF.Identity, scale=kdec[:, h:h + 1]),
                    waits=[(s_pe, s_pe.n), (s_pe, ktd_free[i4])],
                    inc=s_evu)
            tp_free[i2] = s_evu.n
            ktoks.append(s_evu.n)
        for hh in range(4):
            h = es * 4 + hh
            i4 = (base + hh) % 4
            pai[0] += 1

            def umm(e, tch=tch, h=h, i4=i4, nktd=ktoks[hh], nu=u_free[i4],
                    nv=v_tok[(es, tch)],
                    nrot=(qsw_free[i4 - 2] if i4 >= 2 else None)):
                e.wait_ge(s_evu.h, max(nktd, nu))
                e.wait_ge(s_evs.h, nv)
                if nrot:
                    e.wait_ge(nrot[0].h, nrot[1])
                return e.matmul(u_bank(i4), ktd[i4][:],
                                v_sb[:, tch * E + h * 128:tch * E + h * 128 + 128],
                                start=True, stop=True)
            bd.emit("tensor", umm, inc=s_pe)
            ktd_free[i4] = s_pe.n
            dst = (kvsv[:, ((tch + 1) * NH + h) * 128:((tch + 1) * NH + h) * 128 + 128]
                   if tch < NCHK - 1 else u7[:, h * 128:h * 128 + 128])
            bd.emit("scalar", OP("copy", dst, u_bank(i4)),
                    waits=[(s_pe, s_pe.n)], inc=s_evu)
            u_free[i4] = s_evu.n
            pa_tok[(tch, h)] = s_evu.n

    def phase_b(es):
        for h in range(es * 4, es * 4 + 4):
            for c in range(1, NCHK):
                o0 = (c * NH + h) * 128
                p0 = ((c - 1) * NH + h) * 128
                bd.emit("vector", OP("scalar_tensor_tensor",
                                     kvsv[:, o0:o0 + 128],
                                     kvsv[:, p0:p0 + 128], float(rC[h]),
                                     kvsv[:, o0:o0 + 128], ALU.mult, ALU.add),
                        waits=[(s_evu, pa_tok[(c - 1, h)]),
                               (s_ev, kvsv0_tok[0]), vch], inc=vch)
            o7 = ((NCHK - 1) * NH + h) * 128
            bd.emit("vector", OP("scalar_tensor_tensor",
                                 kvf[:, h * 128:h * 128 + 128],
                                 kvsv[:, o7:o7 + 128], float(rC[h]),
                                 u7[:, h * 128:h * 128 + 128],
                                 ALU.mult, ALU.add),
                    waits=[(s_evu, pa_tok[(NCHK - 1, h)]), vch], inc=s_ev)

    for es in range(4):
        bi = es % 2
        bd.emit("gpsimd", OP("dma_start",
                             out=bigbuf[bi][:].rearrange("p (c j) -> p c j", c=KC),
                             in_=wv5[es]),
                waits=[(s_pe, vb_used[bi])], inc=(s_vb[bi], 16))
        for tch in range(NCHK):
            if tch >= 2:
                phase_a(es, tch - 2)
            i2 = (es * NCHK + tch) % 2

            def vmm(e, bi=bi, tch=tch, i2=i2, nvb=s_vb[bi].n, nev=pv_free[i2]):
                e.wait_ge(s_vb[bi].h, nvb)
                e.wait_ge(s_h.h, n_h_tile[0] if tch < 4 else n_h_tile[1])
                if nev:
                    e.wait_ge(s_evs.h, nev)
                ins = None
                for k in range(KC):
                    ins = e.matmul(pq[i2][:],
                                   h_bf[:, k * SEG + tch * 128:
                                        k * SEG + tch * 128 + 128],
                                   bigbuf[bi][:, k * 512:(k + 1) * 512],
                                   start=(k == 0), stop=(k == KC - 1))
                return ins
            bd.emit("tensor", vmm, inc=s_pe)
            bd.emit("scalar", OP("copy",
                                 v_sb[:, tch * E + es * 512:tch * E + es * 512 + 512],
                                 pq[i2][:]),
                    waits=[(s_pe, s_pe.n)], inc=s_evs)
            pv_free[i2] = s_evs.n
            v_tok[(es, tch)] = s_evs.n
            if tch == 0 and es == 0:
                bd.emit("vector", OP("memset", kvsv[:, 0:NH * HD], 0.0),
                        waits=[vch], inc=s_ev)
                kvsv0_tok[0] = s_ev.n
                bd.emit("vector", OP("memset", kvin[:], 0.0), inc=vch)
        phase_a(es, NCHK - 2)
        phase_a(es, NCHK - 1)
        phase_b(es)
        bd.emit("sync", OP("dma_start",
                           out=agi.ap()[:, es * 512:(es + 1) * 512],
                           in_=kvf[:, es * 512:(es + 1) * 512]),
                waits=[(s_ev, s_ev.n)], inc=(s_ag, 16))
        vb_used[bi] = s_pe.n
    n_v_all = s_evs.n
    n_scan = s_ev.n
    qsw_free[0] = (s_evu, u_free[2])
    qsw_free[1] = (s_evu, u_free[3])

    for ms in range(2):
        bi = ms % 2
        bd.emit("gpsimd", OP("dma_start",
                             out=wbuf[bi][:].rearrange("p (c m) -> p c m", c=KC),
                             in_=wq5[ms]),
                waits=[(s_pe, wb_used[bi])], inc=(s_wb[bi], 16))

    def coll(e):
        return e.collective_compute(
            "AllGather", ALU.bypass,
            replica_groups=[[0, 1, 2, 3], [4, 5, 6, 7]],
            ins=[agi.ap().opt()], outs=[ago.ap().opt()])
    bd.emit("gpsimd", coll, waits=[(s_ag, 64)], inc=s_cc)

    ctx_bb.close()
    q_sb = SB("q_sb", [128, NH * SEG], BF16, ctx34)
    ctx_u7.close()
    lprev = SB("lprev", [128, NH * HD], BF16, ctx3o)

    proj_rot(wq5, q_sb, "q", skip_dma=2)

    for s2 in range(3):
        bd.emit("gpsimd", OP("dma_start", out=lprev[:],
                             in_=ago.ap()[s2 * 128:(s2 + 1) * 128, :]),
                waits=[(s_cc, 1), (s_kvin, s_kvin.n)], inc=(s_lp, 16))
        for h in range(NH):
            bd.emit("vector", OP("scalar_tensor_tensor",
                                 kvin[:, h * 128:h * 128 + 128],
                                 lprev[:, h * 128:h * 128 + 128],
                                 dsg[:, s2 * NH + h:s2 * NH + h + 1],
                                 kvin[:, h * 128:h * 128 + 128],
                                 ALU.mult, ALU.add),
                    waits=[(s_lp, 16 * (s2 + 1)), vch], inc=s_kvin)
    n_kvin = s_kvin.n

    bd.flush_block()
    ctx_ps3.close()
    ctx_ps2.close()
    ctx23.close()
    ctx3o.close()
    ctx_rot.close()

    ctx4 = ExitStack()
    attn = SB("attn", [128, NH * SEG], BF16, ctx4)
    ctx4s = ExitStack()
    dmt_sb = SB("dmt_sb", [128, NH * 128], BF16, ctx4s)
    qdb_sb = SB("qdb_sb", [128, NH * 128], BF16, ctx4s)
    rcb_sb = SB("rcb_sb", [128, NH * 128], BF16, ctx4s)
    kvrun = SB("kvrun", [128, NH * 128], BF16, ctx4s)
    kvt4 = [SB(f"kvt4_{i}", [128, 512], BF16, ctx4s) for i in range(2)]
    qef = [SB(f"qef{i}", [128, 128], BF16, ctx4s) for i in range(4)]
    smx = [SB(f"smx{i}", [128, 128], BF16, ctx4s) for i in range(4)]

    ctx_ps4 = ExitStack()
    S_ps = [PSUM(f"S{i}", [128, 128], F32, ctx_ps4) for i in range(4)]
    o_ps = [PSUM(f"o{i}", [128, 128], F32, ctx_ps4) for i in range(4)]

    bd.emit("sync", OP("dma_start",
                       out=dmt_sb[:].rearrange("p (h i) -> p h i", h=NH), in_=dmt5),
            inc=(s_in, 16))
    bd.emit("sync", OP("dma_start",
                       out=qdb_sb[:].rearrange("p (h i) -> p h i", h=NH), in_=qdb5),
            inc=(s_in, 16))
    bd.emit("sync", OP("dma_start",
                       out=rcb_sb[:].rearrange("p (h i) -> p h i", h=NH), in_=rcb5),
            inc=(s_in, 16))
    n_in4 = s_in.n

    kvt4_free = [0, 0]
    qef_free = [0] * 4
    smx_free = [0] * 4
    S_free = [0] * 4
    o_free = [0] * 4
    attn_tok = {}
    bd.emit("vector", OP("tensor_copy", kvrun[:], kvin[:]),
            waits=[(s_kvin, n_kvin), vch], inc=vch)
    for c in range(NCHK):
        if c > 0:
            bd.emit("vector", OP("tensor_mul", kvrun[:], kvrun[:], rcb_sb[:]),
                    waits=[(s_in, n_in4), vch], inc=vch)
        for hg in range(4):
            g2 = (c * 4 + hg) % 2
            bd.emit("vector", OP("tensor_add", kvt4[g2][:],
                                 kvrun[:, hg * 512:hg * 512 + 512],
                                 kvsv[:, c * NH * HD + hg * 512:
                                      c * NH * HD + hg * 512 + 512]),
                    waits=[(s_pe, kvt4_free[g2]), vch], inc=vch)
            n_kvt = vch.n
            for hh in range(4):
                h = hg * 4 + hh
                i2 = (c * NH + h) % 4
                ks = h * SEG + c * 128
                bd.emit("vector", OP("tensor_mul", qef[i2][:],
                                     q_sb[:, ks:ks + 128],
                                     qdb_sb[:, h * 128:h * 128 + 128]),
                        waits=[(s_in, n_in4), (s_pe, qef_free[i2]),
                               (s_ev, rot_ready[("q", h, c // 4)]), vch],
                        inc=vch)
                n_qef = vch.n

                def smm(e, ks=ks, i2=i2, nfree=S_free[i2],
                        nq=rot_ready[("q", h, c // 4)]):
                    e.wait_ge(s_ev.h, nq)
                    if nfree:
                        e.wait_ge(vch.h, nfree)
                    return e.matmul(S_ps[i2][:], k_sb[:, ks:ks + 128],
                                    q_sb[:, ks:ks + 128], start=True, stop=True)
                bd.emit("tensor", smm, inc=s_pe)
                n_s = s_pe.n
                bd.emit("vector", OP("tensor_mul", smx[i2][:], S_ps[i2][:],
                                     dmt_sb[:, h * 128:h * 128 + 128]),
                        waits=[(s_pe, n_s), (s_pe, smx_free[i2]),
                               (s_in, n_in4), vch],
                        inc=vch)
                S_free[i2] = vch.n
                n_sm = vch.n

                def omm(e, c=c, h=h, hh=hh, g2=g2, i2=i2, n_sm=n_sm,
                        n_kvt=n_kvt, n_qef=n_qef, noe=o_free[i2]):
                    e.wait_ge(vch.h, max(n_sm, n_kvt, n_qef))
                    if noe:
                        e.wait_ge(sch.h, noe)
                    e.matmul(o_ps[i2][:],
                             v_sb[:, c * E + h * 128:c * E + h * 128 + 128],
                             smx[i2][:], start=True, stop=False)
                    return e.matmul(o_ps[i2][:],
                                    kvt4[g2][:, hh * 128:hh * 128 + 128],
                                    qef[i2][:], start=False, stop=True)
                bd.emit("tensor", omm, inc=s_pe)
                kvt4_free[g2] = s_pe.n
                qef_free[i2] = s_pe.n
                smx_free[i2] = s_pe.n
                bd.emit("scalar", OP("copy", attn[:, ks:ks + 128], o_ps[i2][:]),
                        waits=[(s_pe, s_pe.n), sch], inc=sch)
                o_free[i2] = sch.n

    bd.flush_block()
    ctx_ps4.close()
    ctx24.close()
    ctx34.close()
    ctx4s.close()

    ctx56 = ExitStack()
    x_mid = SB("x_mid", [128, KC * SEG], F32, ctx56)
    RB = mk_rows(ctx56, "b")
    ctx5 = ExitStack()
    attn_n = SB("attn_n", [128, NH * SEG], BF16, ctx5)
    xsq5b = [SB(f"xsq5b{i}", [128, 512], BF16, ctx5) for i in range(2)]
    xres = [SB(f"xres{i}", [128, 512], F32, ctx5) for i in range(2)]
    ctx67 = ExitStack()
    h2_bf = SB("h2_bf", [128, KC * SEG], BF16, ctx67)
    ctx6 = ExitStack()
    xb6 = [SB(f"xb6{i}", [128, 512], BF16, ctx6) for i in range(2)]
    xq6 = [SB(f"xq6{i}", [128, 512], BF16, ctx6) for i in range(2)]
    t16 = SB("t16", [128, 512], F32, ctx6)
    t26 = SB("t26", [128, 512], F32, ctx6)

    ctx_ps5 = ExitStack()
    po = [PSUM(f"po{i}", [128, 512], F32, ctx_ps5) for i in range(2)]
    ctx_ps5b = ExitStack()
    ssq5 = PSUM("ssq5", [1, 512], F32, ctx_ps5b)
    A5 = [PSUM(f"A5{i}", [128, 512], F32, ctx_ps5b) for i in range(2)]

    s_xr = [bd.sem() for _ in range(2)]
    s_xm = bd.sem()

    n_row0 = s_row.n
    pair5 = {}
    for tile in range(2):
        for k in range(KC):
            bi = k % 2
            c0 = k * SEG + tile * 512
            bd.emit("scalar", OP("activation", xsq5b[bi][:], attn[:, c0:c0 + 512],
                                 AF.Square),
                    waits=[(sch, sch.n),
                           (s_pe, pair5.get((tile, k - 2),
                                            pair5.get((tile - 1, KC - 1), 0)))],
                    inc=s_sq)
            nsq = s_sq.n

            def stat5k(e, k=k, nsq=nsq):
                e.wait_ge(s_sq.h, nsq)
                return e.matmul(ssq5[:], onesb[:], xsq5b[k % 2][:],
                                start=(k == 0), stop=(k == KC - 1))
            w = [s_vz]
            if k == 0:
                w.append((s_row, s_row.n))
            bd.emit("tensor", stat5k, waits=w, inc=s_pe)
            pair5[(tile, k)] = s_pe.n
        rowmath(RB, None, ssq5, tile, s_pe.n, E, rms_only=True)
        bd.emit("tensor", OP("matmul", A5[tile][:], onesf[:], RB["is_"][tile][:],
                             start=True, stop=True),
                waits=[(s_row, n_row0 + tile + 1)], inc=s_pe)
        nA = s_pe.n
        for k in range(KC):
            c0 = k * SEG + tile * 512
            bd.emit("vector", OP("scalar_tensor_tensor", attn_n[:, c0:c0 + 512],
                                 attn[:, c0:c0 + 512], rmsw[:, k:k + 1],
                                 A5[tile][:], ALU.mult, ALU.mult),
                    waits=[(s_pe, nA), (s_in, n_in1), vch], inc=s_h)
    n_attn_n = s_h.n
    ctx_ps5b.close()

    ctx_ps6 = ExitStack()
    ssum6 = [PSUM(f"ssum6_{i}", [1, 512], F32, ctx_ps6) for i in range(2)]
    ssq6 = [PSUM(f"ssq6_{i}", [1, 512], F32, ctx_ps6) for i in range(2)]
    A6 = PSUM("A6", [128, 512], F32, ctx_ps6)
    B6 = PSUM("B6", [128, 512], F32, ctx_ps6)

    s_h2 = bd.sem()
    s_spl = bd.sem()
    n_row5e = s_row.n

    po_free = [0, 0]
    xr_used = [0, 0]
    xm_tok = {}
    pair6 = {}
    for ms in range(16):
        bi = ms % 2
        bd.emit("gpsimd", OP("dma_start",
                             out=wbuf[bi][:].rearrange("p (c m) -> p c m", c=KC),
                             in_=wo5[ms]),
                waits=[(s_pe, wb_used[bi])], inc=(s_wb[bi], 16))
        for tile in range(2):
            i2 = (ms * 2 + tile) % 2
            bd.emit("sync", OP("dma_start", out=xres[i2][:],
                               in_=x5[:, ms:ms + 1,
                                      tile * 512:tile * 512 + 512]
                               .rearrange("p one t -> p (one t)")),
                    waits=[(s_xm, xr_used[i2])], inc=(s_xr[i2], 16))
            nxr = s_xr[i2].n

            def womm(e, bi=bi, tile=tile, i2=i2, nwb=s_wb[bi].n,
                     nev=po_free[i2]):
                e.wait_ge(s_wb[bi].h, nwb)
                e.wait_ge(s_h.h, n_attn_n)
                if nev:
                    e.wait_ge(s_xm.h, nev)
                ins = None
                for k in range(KC):
                    ins = e.matmul(po[i2][:], wbuf[bi][:, k * 128:(k + 1) * 128],
                                   attn_n[:, k * SEG + tile * 512:
                                          k * SEG + tile * 512 + 512],
                                   start=(k == 0), stop=(k == KC - 1))
                return ins
            bd.emit("tensor", womm, inc=s_pe)
            c0 = ms * SEG + tile * 512
            bd.emit("vector", OP("scalar_tensor_tensor", x_mid[:, c0:c0 + 512],
                                 po[i2][:], gav[:, ms:ms + 1], xres[i2][:],
                                 ALU.mult, ALU.add),
                    waits=[(s_pe, s_pe.n), (s_xr[i2], nxr), (s_mv, n_modv),
                           vch],
                    inc=s_xm)
            po_free[i2] = s_xm.n
            xr_used[i2] = s_xm.n
            xm_tok[(ms, tile)] = s_xm.n
        wb_used[bi] = s_pe.n
        k = ms
        for tile in range(2):
            b6 = tile
            c0 = k * SEG + tile * 512
            ptok = pair6.get((tile, k - 1), 0)
            bd.emit("vector", OP("tensor_copy", xb6[b6][:], x_mid[:, c0:c0 + 512]),
                    waits=[(s_xm, xm_tok[(k, tile)]), (s_pe, ptok)], inc=vch)
            ncp = vch.n
            bd.emit("scalar", OP("activation", xq6[b6][:], x_mid[:, c0:c0 + 512],
                                 AF.Square),
                    waits=[(s_xm, xm_tok[(k, tile)]), (s_pe, ptok)], inc=s_sq)
            nsq = s_sq.n

            def stat6k(e, k=k, b6=b6, tile=tile, ncp=ncp, nsq=nsq):
                e.wait_ge(vch.h, ncp)
                e.wait_ge(s_sq.h, nsq)
                if k == 0:
                    e.wait_ge(s_h.h, n_attn_n)
                    e.wait_ge(s_row.h, n_row5e)
                e.matmul(ssum6[tile][:], onesb[:], xb6[b6][:],
                         start=(k == 0), stop=(k == KC - 1))
                return e.matmul(ssq6[tile][:], onesb[:], xq6[b6][:],
                                start=(k == 0), stop=(k == KC - 1))
            bd.emit("tensor", stat6k, waits=[s_vz], inc=s_pe)
            pair6[(tile, k)] = s_pe.n
    n_xmid = s_xm.n

    bd.emit("sync", OP("dma_start", out=xmd.ap(),
                       in_=x_mid[:].rearrange("p (c t) -> p c t", c=KC)),
            waits=[(s_xm, n_xmid)], inc=(s_spl, 16))
    n_row6 = s_row.n
    for tile in range(2):
        rowmath(RB, ssum6[tile], ssq6[tile], tile, pair6[(tile, KC - 1)], E)
        bd.emit("tensor", OP("matmul", A6[:], onesf[:], RB["is_"][tile][:],
                             start=True, stop=True),
                waits=[(s_row, n_row6 + tile + 1), (s_h2, s_h2.n)], inc=s_pe)
        bd.emit("tensor", OP("matmul", B6[:], onesf[:], RB["b"][tile][:],
                             start=True, stop=True), inc=s_pe)
        nab = s_pe.n
        for k in range(KC):
            c0 = k * SEG + tile * 512
            bd.emit("vector", OP("tensor_mul", t16[:], x_mid[:, c0:c0 + 512],
                                 A6[:]),
                    waits=[(s_pe, nab), vch], inc=vch)
            bd.emit("vector", OP("tensor_add", t26[:], t16[:], B6[:]),
                    waits=[vch], inc=vch)
            bd.emit("vector", OP("tensor_scalar", h2_bf[:, c0:c0 + 512], t26[:],
                                 cb1[:, k:k + 1], sbv[:, k:k + 1],
                                 ALU.mult, ALU.add),
                    waits=[vch], inc=s_h2)
    n_h2_all = s_h2.n
    n_h2_half = [16, 32]
    ctx_ps6.close()
    ctx6.close()
    ctx5.close()
    ctx4.close()
    ctx56.close()
    ctx_ps5.close()

    ctx7 = ExitStack()
    y_acc = SB("y_acc", [128, KC * SEG], BF16, ctx7)
    tadd = SB("tadd", [128, 512], F32, ctx7)
    xr7 = [SB(f"xr7{i}", [128, 512], F32, ctx7) for i in range(2)]
    oscr = [SB(f"oscr{i}", [128, 512], F32, ctx7) for i in range(2)]

    ctx_rt = ExitStack()
    crow = SB("crow", [1, NE * SEG], BF16, ctx_rt)
    lgrow = SB("lgrow", [8, 512], F32, ctx_rt)
    ltm = SB("ltm", [128, 8], F32, ctx_rt)
    rmax = SB("rmax", [128, 1], F32, ctx_rt)
    exv = SB("exv", [128, 8], F32, ctx_rt)
    sumv = SB("sumv", [128, 1], F32, ctx_rt)
    rsum = SB("rsum", [128, 1], F32, ctx_rt)
    scor = SB("scor", [128, 8], F32, ctx_rt)
    mx8 = SB("mx8", [128, 8], F32, ctx_rt)
    msk = SB("msk", [128, 8], F32, ctx_rt)
    ctm = SB("ctm", [128, 8], F32, ctx_rt)

    ctx_wv = ExitStack()
    comb = SB("comb", [128, NE * SEG], BF16, ctx_wv)
    gsil = SB("gsil", [128, FC * SEG], BF16, ctx_wv)
    act_w = [SB(f"act_w{i}", [128, FC * SEG], BF16, ctx_wv) for i in range(2)]
    u1s = [SB(f"u1s{i}", [128, 512], BF16, ctx_wv) for i in range(2)]
    dnb = [SB(f"dnb{i}", [128, FC * 128], BF16, ctx_wv) for i in range(4)]

    ctx_ps7 = ExitStack()
    y_ps = PSUM("y_ps", [128, 2048], F32, ctx_ps7)
    gu_ps = [PSUM(f"gu{i}", [128, 512], F32, ctx_ps7) for i in range(2)]
    rt_ps = PSUM("rt_ps", [8, 512], F32, ctx_ps7)
    lt_ps = PSUM("lt_ps", [128, 8], F32, ctx_ps7)

    s_gs = bd.sem()
    s_act = bd.sem()
    s_ye = bd.sem()
    s_xr7 = [bd.sem() for _ in range(2)]
    s_oev = bd.sem()
    s_sto = [bd.sem() for _ in range(2)]

    bd.inject_all((s_spl, 16))

    comb_tok = [0]
    def emit_routing():
        rt_free = 0
        lt_free = 0
        cr_free = 0
        for half in range(2):
            def lgm(e, half=half, nrt=max(rt_free, cr_free)):
                e.wait_ge(s_h2.h, n_h2_all)
                e.wait_ge(s_gin.h, n_gin1)
                if nrt:
                    e.wait_ge(vch.h, nrt)
                ins = None
                for k in range(KC):
                    ins = e.matmul(rt_ps[:], gw_sb[:, k * 8:(k + 1) * 8],
                                   h2_bf[:, k * SEG + half * 512:
                                         k * SEG + half * 512 + 512],
                                   start=(k == 0), stop=(k == KC - 1))
                return ins
            bd.emit("tensor", lgm, inc=s_pe)
            bd.emit("vector", OP("tensor_copy", lgrow[:], rt_ps[:]),
                    waits=[(s_pe, s_pe.n), vch], inc=vch)
            rt_free = vch.n
            n_lgrow = vch.n
            for sub in range(4):
                def ltr(e, sub=sub, nlg=n_lgrow, nlt=lt_free):
                    e.wait_ge(vch.h, max(nlg, nlt))
                    return e.transpose(lt_ps[:], lgrow[:, sub * 128:(sub + 1) * 128],
                                       identf[0:8, 0:8])
                bd.emit("tensor", ltr, waits=[(s_in, n_in1)], inc=s_pe)
                bd.emit("vector", OP("tensor_copy", ltm[:], lt_ps[:]),
                        waits=[(s_pe, s_pe.n), vch], inc=vch)
                lt_free = vch.n
                bd.emit("vector", OP("reduce_max", rmax[:], ltm[:], AX.X),
                        waits=[vch], inc=vch)
                bd.emit("vector", OP("tensor_scalar_mul", rmax[:], rmax[:], -1.0),
                        waits=[vch], inc=vch)
                bd.emit("scalar", OP("activation", exv[:], ltm[:], AF.Exp,
                                     bias=rmax[:, 0:1]),
                        waits=[vch, sch], inc=sch)
                bd.emit("vector", OP("reduce_sum", sumv[:], exv[:], AX.X),
                        waits=[sch, vch], inc=vch)
                bd.emit("vector", OP("reciprocal", rsum[:], sumv[:]),
                        waits=[vch], inc=vch)
                bd.emit("vector", OP("tensor_scalar_mul", scor[:], exv[:],
                                     rsum[:, 0:1]), waits=[vch], inc=vch)
                bd.emit("vector", OP("max", out=mx8[:], in_=scor[:]),
                        waits=[vch], inc=vch)
                bd.emit("vector", OP("tensor_scalar", msk[:], scor[:],
                                     mx8[:, 1:2], None, ALU.is_ge),
                        waits=[vch], inc=vch)
                bd.emit("vector", OP("tensor_mul", ctm[:], scor[:], msk[:]),
                        waits=[vch], inc=vch)
                n_ctm = vch.n
                for e2 in range(NE):
                    def ctr(e, e2=e2, nctm=n_ctm, ncr=cr_free):
                        e.wait_ge(vch.h, max(nctm, ncr))
                        return e.transpose(rt_ps[0:1, 0:128], ctm[:, e2:e2 + 1],
                                           identf[:])
                    bd.emit("tensor", ctr, inc=s_pe)
                    c0 = e2 * SEG + half * 512 + sub * 128
                    bd.emit("vector", OP("tensor_copy", crow[:, c0:c0 + 128],
                                         rt_ps[0:1, 0:128]),
                            waits=[(s_pe, s_pe.n), vch], inc=vch)
                    cr_free = vch.n
        n_crow = vch.n
        cb_free = 0
        for e2 in range(NE):
            for half in range(2):
                c0 = e2 * SEG + half * 512

                def cbm(e, c0=c0, ncb=cb_free):
                    e.wait_ge(vch.h, n_crow)
                    if ncb:
                        e.wait_ge(vch.h, ncb)
                    return e.matmul(y_ps[:, 0:512], ones1b[:], crow[:, c0:c0 + 512],
                                    start=True, stop=True)
                bd.emit("tensor", cbm, waits=[s_vz], inc=s_pe)
                bd.emit("vector", OP("tensor_copy", comb[:, c0:c0 + 512],
                                     y_ps[:, 0:512]),
                        waits=[(s_pe, s_pe.n), vch], inc=vch)
                cb_free = vch.n
            comb_tok[0] = vch.n


    emit_routing()

    gu_free = [None, None]
    act_ready = {}
    gs_ready = {}
    gs_cons = {}
    y_bank = {}
    wb7_used = list(wb_used)
    db_used = [0] * 4
    for w in range(4):
        e_pair = (2 * w, 2 * w + 1)
        for ex in e_pair:
            ai = ex % 2
            for ms in range(GUC):
                bi = ms % 2
                bd.emit("gpsimd", OP("dma_start",
                                     out=wbuf[bi][:].rearrange("p (c m) -> p c m",
                                                               c=KC),
                                     in_=gup5[ex, ms]),
                        waits=[(s_pe, wb7_used[bi])], inc=(s_wb[bi], 16))
                for half in range(2):
                    i2 = (ms * 2 + half) % 2

                    def gmm(e, bi=bi, half=half, i2=i2, nwb=s_wb[bi].n,
                            ngu=gu_free[i2]):
                        e.wait_ge(s_wb[bi].h, nwb)
                        e.wait_ge(s_h2.h, n_h2_half[half])
                        if ngu:
                            e.wait_ge(ngu[0].h, ngu[1])
                        ins = None
                        for k in range(KC):
                            ins = e.matmul(gu_ps[i2][:],
                                           wbuf[bi][:, k * 128:(k + 1) * 128],
                                           h2_bf[:, k * SEG + half * 512:
                                                 k * SEG + half * 512 + 512],
                                           start=(k == 0), stop=(k == KC - 1))
                        return ins
                    bd.emit("tensor", gmm, inc=s_pe)
                    npe = s_pe.n
                    c0 = (ms % FC) * SEG + half * 512
                    if ms < FC:
                        bd.emit("scalar", OP("activation", gsil[:, c0:c0 + 512],
                                             gu_ps[i2][:], AF.Silu),
                                waits=[(s_pe, npe), sch,
                                       (s_act, gs_cons.get((ms, half), 0))],
                                inc=s_gs)
                        gs_ready[(ex, ms, half)] = s_gs.n
                        gu_free[i2] = (s_gs, s_gs.n)
                    else:
                        fc = ms - FC
                        bd.emit("vector", OP("tensor_mul", u1s[i2][:],
                                             gu_ps[i2][:],
                                             comb[:, ex * SEG + half * 512:
                                                  ex * SEG + half * 512 + 512]),
                                waits=[(s_pe, npe), (vch, comb_tok[0]), vch],
                                inc=s_ev)
                        gu_free[i2] = (s_ev, s_ev.n)
                        bd.emit("vector", OP("tensor_mul",
                                             act_w[ai][:, c0:c0 + 512],
                                             u1s[i2][:], gsil[:, c0:c0 + 512]),
                                waits=[(s_gs, gs_ready[(ex, fc, half)]), vch],
                                inc=s_act)
                        act_ready[(ex, fc, half)] = s_act.n
                        gs_cons[(fc, half)] = s_act.n
                wb7_used[bi] = s_pe.n
        dn_mm_done = {}
        for mh in range(KC):
            dbb = (mh % 2) * 2
            for ei, ex in enumerate(e_pair):
                bd.emit("gpsimd", OP("dma_start",
                                     out=dnb[dbb + ei][:].rearrange(
                                         "p (c m) -> p c m", c=FC),
                                     in_=dn5[ex, mh]),
                        waits=[(s_pe, dn_mm_done.get(mh - 2, db_used[0]))],
                        inc=(s_db[dbb + ei], 16))
            ndb = (s_db[dbb].n, s_db[dbb + 1].n)
            for half in range(2):
                y0 = ((mh % 2) * 2 + half) * 512

                def dmm(e, e_pair=e_pair, half=half, y0=y0, ndb=ndb, dbb=dbb,
                        nacts=max(act_ready[(ex, fc, half)]
                                  for ex in e_pair for fc in range(FC)),
                        nye=y_bank.get((y0 // 512), 0)):
                    e.wait_ge(s_db[dbb].h, ndb[0])
                    e.wait_ge(s_db[dbb + 1].h, ndb[1])
                    e.wait_ge(s_act.h, nacts)
                    if nye:
                        e.wait_ge(s_ye.h, nye)
                    ins = None
                    n = 0
                    for ei2, ex in enumerate(e_pair):
                        for k in range(FC):
                            ins = e.matmul(y_ps[:, y0:y0 + 512],
                                           dnb[dbb + ei2][:, k * 128:(k + 1) * 128],
                                           act_w[ex % 2][:, k * SEG + half * 512:
                                                          k * SEG + half * 512 + 512],
                                           start=(n == 0), stop=(n == 2 * FC - 1))
                            n += 1
                    return ins
                bd.emit("tensor", dmm, inc=s_pe)
                c0 = mh * SEG + half * 512
                if w == 0:
                    bd.emit("vector", OP("tensor_copy", y_acc[:, c0:c0 + 512],
                                         y_ps[:, y0:y0 + 512]),
                            waits=[(s_pe, s_pe.n), vch], inc=s_ye)
                else:
                    bd.emit("vector", OP("tensor_add", y_acc[:, c0:c0 + 512],
                                         y_acc[:, c0:c0 + 512],
                                         y_ps[:, y0:y0 + 512]),
                            waits=[(s_pe, s_pe.n), vch], inc=s_ye)
                y_bank[(y0 // 512)] = s_ye.n
            dn_mm_done[mh] = s_pe.n
        db_used = [s_pe.n] * 4

    bd.flush_block()
    ctx_rt.close()
    ctx_wv.close()

    ctx_sh = ExitStack()
    ssil = SB("ssil", [128, FSC * SEG], BF16, ctx_sh)
    sact = SB("sact", [128, FSC * SEG], BF16, ctx_sh)
    sdnb = [SB(f"sdnb{i}", [128, FSC * 128], BF16, ctx_sh) for i in range(4)]

    sgs_ready = {}
    sact_ready = {}
    for ms in range(SGUC):
        bi = ms % 2
        bd.emit("gpsimd", OP("dma_start",
                             out=wbuf[bi][:].rearrange("p (c m) -> p c m", c=KC),
                             in_=sgu5[ms]),
                waits=[(s_pe, wb7_used[bi])], inc=(s_wb[bi], 16))
        for half in range(2):
            i2 = (ms * 2 + half) % 2

            def sgm(e, bi=bi, half=half, i2=i2, nwb=s_wb[bi].n,
                    ngu=gu_free[i2]):
                e.wait_ge(s_wb[bi].h, nwb)
                e.wait_ge(s_h2.h, n_h2_half[half])
                if ngu:
                    e.wait_ge(ngu[0].h, ngu[1])
                ins = None
                for k in range(KC):
                    ins = e.matmul(gu_ps[i2][:],
                                   wbuf[bi][:, k * 128:(k + 1) * 128],
                                   h2_bf[:, k * SEG + half * 512:
                                         k * SEG + half * 512 + 512],
                                   start=(k == 0), stop=(k == KC - 1))
                return ins
            bd.emit("tensor", sgm, inc=s_pe)
            npe = s_pe.n
            c0 = (ms % FSC) * SEG + half * 512
            if ms < FSC:
                bd.emit("scalar", OP("activation", ssil[:, c0:c0 + 512],
                                     gu_ps[i2][:], AF.Silu),
                        waits=[(s_pe, npe), sch], inc=s_gs)
                sgs_ready[(ms, half)] = s_gs.n
                gu_free[i2] = (s_gs, s_gs.n)
            else:
                fc = ms - FSC
                bd.emit("vector", OP("tensor_mul", sact[:, c0:c0 + 512],
                                     gu_ps[i2][:], ssil[:, c0:c0 + 512]),
                        waits=[(s_pe, npe), (s_gs, sgs_ready[(fc, half)]), vch],
                        inc=s_act)
                sact_ready[(fc, half)] = s_act.n
                gu_free[i2] = (s_act, s_act.n)
        wb7_used[bi] = s_pe.n

    sdn_mm_done = {}
    xr7_free = [0, 0]
    osc_free = [0, 0]
    for mh in range(KC):
        bi = mh % 4
        bd.emit("gpsimd", OP("dma_start",
                             out=sdnb[bi][:].rearrange("p (c m) -> p c m", c=FSC),
                             in_=sdn5[mh]),
                waits=[(s_pe, sdn_mm_done.get(mh - 4, db_used[0]))],
                inc=(s_db[bi], 16))
        for half in range(2):
            y0 = ((mh % 2) * 2 + half) * 512
            i2 = (mh * 2 + half) % 2
            bd.emit("sync", OP("dma_start", out=xr7[i2][:],
                               in_=xmd.ap()[:, mh:mh + 1,
                                            half * 512:half * 512 + 512]
                               .rearrange("p one t -> p (one t)")),
                    waits=[(s_spl, 16), (s_oev, xr7_free[i2])],
                    inc=(s_xr7[i2], 16))
            nxr = s_xr7[i2].n

            def sdm(e, bi=bi, half=half, y0=y0, ndb=s_db[bi].n,
                    nacts=max(sact_ready[(fc, half)] for fc in range(FSC)),
                    nye=y_bank.get((y0 // 512), 0)):
                e.wait_ge(s_db[bi].h, ndb)
                e.wait_ge(s_act.h, nacts)
                if nye:
                    e.wait_ge(s_ye.h, nye)
                ins = None
                for k in range(FSC):
                    ins = e.matmul(y_ps[:, y0:y0 + 512],
                                   sdnb[bi][:, k * 128:(k + 1) * 128],
                                   sact[:, k * SEG + half * 512:
                                        k * SEG + half * 512 + 512],
                                   start=(k == 0), stop=(k == FSC - 1))
                return ins
            bd.emit("tensor", sdm, inc=s_pe)
            c0 = mh * SEG + half * 512
            bd.emit("vector", OP("tensor_add", tadd[:], y_acc[:, c0:c0 + 512],
                                 y_ps[:, y0:y0 + 512]),
                    waits=[(s_pe, s_pe.n), (s_ye, s_ye.n), vch],
                    inc=s_ye)
            y_bank[(y0 // 512)] = s_ye.n
            bd.emit("vector", OP("scalar_tensor_tensor", oscr[i2][:], tadd[:],
                                 gbv[:, mh:mh + 1], xr7[i2][:],
                                 ALU.mult, ALU.add),
                    waits=[(s_xr7[i2], nxr), (s_mv, n_modv),
                           (s_sto[i2], osc_free[i2])],
                    inc=s_oev)
            xr7_free[i2] = s_oev.n
            n_osc = s_oev.n
            bd.emit("sync", OP("dma_start",
                               out=out5[:, mh:mh + 1,
                                        half * 512:half * 512 + 512]
                               .rearrange("p one t -> p (one t)"),
                               in_=oscr[i2][:]),
                    waits=[(s_oev, n_osc)], inc=(s_sto[i2], 16))
            osc_free[i2] = s_sto[i2].n
        sdn_mm_done[mh] = s_pe.n

    n0, n1 = s_sto[0].n, s_sto[1].n

    def fin(e):
        e.wait_ge(s_sto[0].h, n0)
        return e.wait_ge(s_sto[1].h, n1)
    bd.emit("sync", fin)
    bd.flush_block()

    ctx_ps7.close()
    ctx_sh.close()
    ctx7.close()
    ctx67.close()
    gctx.close()
    return nc



def _tables(seg):
    sl = _slopes()
    r = np.exp(-sl)
    t = np.arange(seg * SEG, (seg + 1) * SEG, dtype=np.float64)
    j = np.arange(32, dtype=np.float64)
    freq = (1.0 / 1024.0) ** (j / 31.0)
    th = t[None, :] * freq[:, None]
    ct = np.ones((128, SEG))
    ct[0:32] = np.cos(th)
    ct[64:96] = np.cos(th)
    st = np.zeros((128, SEG))
    st[0:32] = np.sin(th)
    st[64:96] = -np.sin(th)
    ii = np.arange(128, dtype=np.float64)
    diff = ii[None, :] - ii[:, None]
    m = diff >= 0
    dmt = np.zeros((128, NH, 128))
    for h in range(NH):
        dmt[:, h, :] = np.where(m, r[h] ** np.where(m, diff, 0.0), 0.0)
    qd = np.stack([r[h] ** (ii + 1.0) for h in range(NH)], axis=0)
    qdb = np.broadcast_to(qd[None, :, :], (128, NH, 128)).copy()
    kdc = np.stack([r[h] ** (127.0 - ii) for h in range(NH)], axis=1)
    rcb = np.broadcast_to(
        np.repeat(r ** float(CH), 128)[None, :], (128, NH * 128)).copy()
    dsg = np.zeros((128, 3 * NH))
    for s2 in range(3):
        if s2 < seg:
            d = r ** (float(SEG) * (seg - 1 - s2))
            dsg[:, s2 * NH:(s2 + 1) * NH] = d[None, :]
    swp = np.zeros((128, 128))
    for m_ in range(128):
        swp[(m_ + 64) % 128, m_] = 1.0
    bf = lambda a: np.ascontiguousarray(a).astype(ml_dtypes.bfloat16)
    f32 = lambda a: np.ascontiguousarray(a).astype(np.float32)
    return dict(ct=bf(ct), st=bf(st), dmt=bf(dmt), qdb=bf(qdb),
                kdc=f32(kdc), dsg=f32(dsg), rcb=bf(rcb),
                idf=f32(np.eye(128)), idb=bf(np.eye(128)), swb=bf(swp))


def _retile_w(w, n_ms, m_w):
    K, M = w.shape
    kc = K // 128
    return np.ascontiguousarray(
        w.reshape(kc, 128, n_ms, m_w).transpose(2, 1, 0, 3)).astype(np.float32)


_PROG = []


def kernel(L0_attn_x, L0_attn_Wq, L0_attn_Wk, L0_attn_Wv, L0_attn_Wo,
           L0_attn_rms_w, L0_ffn_gate_weight, L0_ffn_gate_up_weights,
           L0_ffn_down_weights, L0_ffn_shared_gate_up, L0_ffn_shared_down,
           L0_c, L0_wmod):
    x = np.asarray(L0_attn_x, dtype=np.float32)
    Wq = np.asarray(L0_attn_Wq, dtype=np.float32)
    Wk = np.asarray(L0_attn_Wk, dtype=np.float32)
    Wv = np.asarray(L0_attn_Wv, dtype=np.float32)
    Wo = np.asarray(L0_attn_Wo, dtype=np.float32)
    rms_w = np.asarray(L0_attn_rms_w, dtype=np.float32)
    gw = np.asarray(L0_ffn_gate_weight, dtype=np.float32)
    gup = np.asarray(L0_ffn_gate_up_weights, dtype=np.float32)
    dn = np.asarray(L0_ffn_down_weights, dtype=np.float32)
    sgu = np.asarray(L0_ffn_shared_gate_up, dtype=np.float32)
    sdn = np.asarray(L0_ffn_shared_down, dtype=np.float32)
    c = np.asarray(L0_c, dtype=np.float32)
    wmod = np.asarray(L0_wmod, dtype=np.float32)

    wq5 = _retile_w(Wq, KC, 128)
    wk5 = _retile_w(Wk, KC, 128)
    wo5 = _retile_w(Wo, KC, 128)
    wv5 = _retile_w(Wv, 4, 512)
    gup5 = np.stack([_retile_w(gup[e].T, GUC, 128) for e in range(NE)])
    dn5 = np.stack([_retile_w(dn[e].T, KC, 128) for e in range(NE)])
    sgu5 = _retile_w(sgu.T, SGUC, 128)
    sdn5 = _retile_w(sdn.T, KC, 128)
    wmod5 = _retile_w(wmod, 24, 512)
    gw5 = np.ascontiguousarray(gw.T.reshape(KC, 128, 8).transpose(1, 0, 2)
                               ).astype(np.float32)
    rmsw5 = np.ascontiguousarray(rms_w.reshape(KC, 128).T).astype(np.float32)

    if not _PROG:
        _PROG.append(build_program())
    nc = _PROG[0]

    in_maps = []
    for core in range(NCORE):
        b, seg = core // 4, core % 4
        xs = x[b, seg * SEG:(seg + 1) * SEG, :]
        x5 = np.ascontiguousarray(xs.T.reshape(KC, 128, SEG).transpose(1, 0, 2)
                                  ).astype(np.float32)
        c_kb = np.ascontiguousarray(c[b].reshape(KC, 128).T).astype(np.float32)
        tb = _tables(seg)
        in_maps.append(dict(
            x5=x5, wq5=wq5, wk5=wk5, wo5=wo5, wv5=wv5, gup5=gup5, dn5=dn5,
            sgu5=sgu5, sdn5=sdn5, wmq=np.ascontiguousarray(
                wmod5[seg * 6:(seg + 1) * 6]),
            gw5=gw5, c_kb=c_kb, rmsw=rmsw5, **tb))

    res = run_bass_kernel_spmd(nc, in_maps, core_ids=list(range(NCORE)))

    out = np.empty((2, 4096, E), dtype=np.float32)
    for core in range(NCORE):
        b, seg = core // 4, core % 4
        o5 = np.asarray(res.results[core]["out5"])
        out[b, seg * SEG:(seg + 1) * SEG, :] = (
            o5.transpose(1, 0, 2).reshape(E, SEG).T)
    return out
